# version 19
# baseline (speedup 1.0000x reference)
import numpy as np
from contextlib import ExitStack

import concourse.bass as bass
import concourse.mybir as mybir
from concourse.bass_utils import run_bass_kernel_spmd

F32 = mybir.dt.float32
BF16 = mybir.dt.bfloat16
AF = mybir.ActivationFunctionType
ALU = mybir.AluOpType
AX = mybir.AxisListType

D = 1024
DC = 8
NCORES = 8
S_FULL = 2048
MODE = "fused"

ENGS = ("t", "v", "s", "g", "y")


class Prog:

    def __init__(self, sems):
        self.sems = sems
        self.cnt = {k: 0 for k in sems}
        self.ops = {e: [] for e in ENGS}
        self.last = {}
        self.hw = {e: {k: 0 for k in sems} for e in ENGS}

    def _wait(self, eng, sem, val):
        if self.hw[eng][sem] >= val:
            return
        self.hw[eng][sem] = val
        h = self.sems[sem]
        self.ops[eng].append(lambda E, h=h, v=val: E.wait_ge(h, v))

    DMA_SEMS = ("ld", "ldw", "ldw2", "hw")

    def need(self, eng, keys):
        for k in keys:
            if k in self.last:
                sem, c = self.last[k]
                if sem in self.DMA_SEMS:
                    c = self.cnt[sem]
                self._wait(eng, sem, c)

    def emit(self, eng, fns, needs=(), inc=None, produces=(), amt=1):
        if not isinstance(fns, (list, tuple)):
            fns = [fns]
        self.need(eng, needs)
        if inc in self.DMA_SEMS:
            self._wait(eng, inc, self.cnt[inc])
        for fn in fns[:-1]:
            self.ops[eng].append(fn)
        if inc is None:
            assert not produces
            self.ops[eng].append(fns[-1])
            return
        self.cnt[inc] += amt
        c = self.cnt[inc]
        h = self.sems[inc]
        self.ops[eng].append(lambda E, fn=fns[-1], h=h, a=amt: fn(E).then_inc(h, a))
        for k in produces:
            self.last[k] = (inc, c)

    def alias(self, dst_key, src_key):
        if src_key in self.last:
            self.last[dst_key] = self.last[src_key]


class KB:
    def __init__(self, S):
        assert S % 512 == 0
        self.S = S
        self.NT4 = S // 512
        self.NT = S // 128
        self.HS = S // 2
        self.nc = bass.Bass("TRN2", target_bir_lowering=False)
        self.es = ExitStack()
        self.inp = {}

    def dram_in(self, name, shape):
        h = self.nc.dram_tensor(name, list(shape), F32, kind="ExternalInput")
        self.inp[name] = h
        return h.ap()

    def dram_out(self, name, shape):
        h = self.nc.dram_tensor(name, list(shape), F32, kind="ExternalOutput")
        return h.ap()

    def alloc(self):
        nc, es, S = self.nc, self.es, self.S
        sb = lambda n, sh, dt: es.enter_context(nc.sbuf_tensor(n, sh, dt))
        ps = lambda n, sh, dt: es.enter_context(nc.psum_tensor(n, sh, dt))
        DCS = DC * S
        self.xs = sb("xs", [128, DCS], BF16)
        self.B = sb("Bx", [128, max(DCS, 16 * 512 + 3 * S + 1024)], BF16)
        self.A = sb("Ax", [128, 2 * DCS + DC * 2048], BF16)
        self.VT = sb("VTx", [128, max(DCS, 16 * 1024)], BF16)
        self.sm = sb("smx", [128, 4096], BF16)
        self.gtb = sb("gtb", [128, 8], BF16)
        self.gsc = sb("gsc", [128, 64], F32)
        self.b1T = sb("b1T", [128, 320], F32)
        self.b2s = sb("b2s", [8, 1024], BF16)
        self.sel = sb("sel_sb", [8, 1024], BF16)
        self.baT = sb("baT", [128, 32], F32)
        self.brow = sb("brow", [1, 8], BF16)
        self.lrow = sb("lrow", [1, 4], BF16)
        self.bgrow = sb("bgrow", [1, 8], BF16)
        self.lwsb = sb("lwsb", [128, 32], BF16)
        self.rwsb = sb("rwsb", [128, 64], BF16)
        self.wgsb = sb("wgsb", [128, 64], BF16)
        self.ones = sb("ones", [128, 128], BF16)
        self.idb = sb("idb", [128, 128], BF16)
        self.idf = sb("idf", [128, 128], F32)
        self.mask = sb("mask_sb", [128, 128], BF16)
        self.PS_A = ps("PS_A", [128, 2048], F32)
        self.PS_B = ps("PS_B", [128, 1024], F32)
        self.PS_T = ps("PS_T", [128, 2048], BF16)
        names = ["pe", "dve", "act", "pool", "ld", "ldw", "ldw2", "hw"]
        sems = {n: es.enter_context(nc.semaphore(n)) for n in names}
        self.P = Prog(sems)
        self.B_f32 = self.B[:, :].bitcast(F32)
        self.A_f32 = self.A[:, :].bitcast(F32)
        self.xs_f32 = self.xs[:, :].bitcast(F32)

    def xs_t(self, dc, c0, w):
        return self.xs[:, dc * self.S + c0: dc * self.S + c0 + w]

    def sb_t(self, dc, c0, w):
        return self.B[:, dc * self.S + c0: dc * self.S + c0 + w]

    def A_t(self, blk, c0, w):
        return self.A[:, blk * self.S + c0: blk * self.S + c0 + w]

    def gfm_t(self, c0, w):
        g0 = 16 * 512 + 2 * self.S + 1024
        return self.B[0:8, g0 + c0: g0 + c0 + w]

    def pT(self, slot):
        return self.PS_T[:, slot * 128:(slot + 1) * 128]


    def p_consts(self, need_mask, need_sel, need_idf):
        P = self.P
        ident = self.dram_in("c_ident", [128, 128])
        P.emit("g", lambda E, s=self, a=ident: E.dma_start(s.idb[:, :], a),
               inc="ld", produces=["idb"], amt=16)
        if need_idf:
            P.emit("g", lambda E, s=self, a=ident: E.dma_start(s.idf[:, :], a),
                   inc="ld", produces=["idf"], amt=16)
        if need_mask:
            m = self.dram_in("c_mask", [128, 128])
            P.emit("g", lambda E, s=self, a=m: E.dma_start(s.mask[:, :], a),
                   inc="ld", produces=["mask"], amt=16)
        if need_sel:
            sl = self.dram_in("c_sel", [8, 1024])
            P.emit("g", lambda E, s=self, a=sl: E.dma_start(s.sel[:, :], a),
                   inc="ld", produces=["sel"], amt=16)
        P.emit("g", lambda E, s=self: E.memset(s.ones[:, :], 1.0),
               inc="pool", produces=["ones"])

    def p_load_x(self):
        P, S = self.P, self.S
        x = self.dram_in("x_in", [S, D])
        for tc in range(self.NT):
            sk = "stg%d" % (tc % 2)
            stg = self.sm[:, (tc % 2) * 1024:(tc % 2) * 1024 + 1024]
            P.emit("g",
                   lambda E, d=stg, a=x[tc * 128:(tc + 1) * 128, :]: E.dma_start(d, a),
                   needs=["f:" + sk], inc="ld", produces=[sk], amt=16)
            bank = tc % 2
            tps, cps = [], []
            for dc in range(8):
                slot = bank * 8 + dc
                tps.append(lambda E, o=self.pT(slot),
                           i=stg[:, dc * 128:(dc + 1) * 128], s=self:
                           E.transpose(o, i, s.idb[:, :]))
                cps.append(lambda E, o=self.xs_t(dc, tc * 128, 128),
                           i=self.pT(slot): E.tensor_copy(o, i))
            P.emit("t", tps, needs=[sk, "idb", "fT%d" % bank,
                                    "fT%dv" % bank, "fT%ds" % bank],
                   inc="pe", produces=["trx"])
            P.emit("v", cps, needs=["trx"], inc="dve",
                   produces=["xs", "fT%d" % bank, "fT%dv" % bank,
                             "fT%ds" % bank])
            P.alias("f:" + sk, "trx")

    def _load_wbuf(self, w_ap, tag, ncols, slot=0):
        P = self.P
        src = w_ap.rearrange("(dc p) m -> p dc m", p=128)
        base = slot * 8192
        dst = self.VT[:, base:base + DC * ncols].rearrange(
            "p (dc m) -> p dc m", dc=DC)
        P.emit("g", lambda E, d=dst, a=src: E.dma_start(d, a),
               needs=["f:wbuf%d" % slot], inc="ldw", produces=[tag], amt=16)

    def p_proj(self, w_ap, tag, src_t, src_key, dst_t=None, dst_key=None,
               bias_col=None, bias_key=None, resid=False, ncols=1024, slot=0,
               out_scale=None):
        P = self.P
        self._load_wbuf(w_ap, tag, ncols, slot=slot)
        base = slot * 8192
        noc = ncols // 128
        for oc in range(noc):
            for nt in range(self.NT4):
                pk = "pB%d" % (oc % 2)
                pslice = self.PS_B[:, (oc % 2) * 512:(oc % 2) * 512 + 512]
                mms = []
                for dc in range(DC):
                    mms.append(lambda E, o=pslice,
                               l=self.VT[:, base + dc * ncols + oc * 128:
                                         base + dc * ncols + oc * 128 + 128],
                               r=src_t(dc, nt * 512, 512),
                               st=(dc == 0), sp=(dc == DC - 1):
                               E.matmul(o, l, r, start=st, stop=sp))
                mk = "mm:%s" % tag
                P.emit("t", mms, needs=[tag, src_key, "f:" + pk],
                       inc="pe", produces=[mk])
                if resid:
                    if bias_col is not None:
                        P.emit("v", lambda E, o=self.sb_t(oc, nt * 512, 512),
                               i=pslice, b=bias_col(oc),
                               x=self.xs_t(oc, nt * 512, 512):
                               E.scalar_tensor_tensor(o, i, b, x, ALU.add, ALU.add),
                               needs=[mk, "xs", bias_key], inc="dve",
                               produces=["sb", "f:" + pk])
                    else:
                        P.emit("v", lambda E, o=self.sb_t(oc, nt * 512, 512),
                               i=pslice, x=self.xs_t(oc, nt * 512, 512):
                               E.tensor_tensor(o, i, x, ALU.add),
                               needs=[mk, "xs"], inc="dve",
                               produces=["sb", "f:" + pk])
                else:
                    if bias_col is not None:
                        P.emit("v", lambda E, o=dst_t(oc, nt * 512, 512),
                               i=pslice, b=bias_col(oc), sc=out_scale:
                               (E.tensor_scalar(o, i, b, None, ALU.add)
                                if sc is None else
                                E.tensor_scalar(o, i, b, sc, ALU.add, ALU.mult)),
                               needs=[mk, bias_key, "sb"], inc="dve",
                               produces=[dst_key, "f:" + pk])
                    else:
                        P.emit("s", lambda E, o=dst_t(oc, nt * 512, 512),
                               i=pslice, sc=out_scale:
                               (E.copy(o, i) if sc is None else
                                E.activation(o, i, AF.Copy, scale=sc)),
                               needs=[mk, "sb"], inc="act",
                               produces=[dst_key, "f:" + pk])
        P.alias("f:wbuf%d" % slot, "mm:%s" % tag)

    def p_build_vt(self, vkey):
        P = self.P
        for kc in range(self.NT):
            bank = kc % 2
            tps, cps = [], []
            for hc in range(8):
                slot = bank * 8 + hc
                tps.append(lambda E, o=self.pT(slot),
                           i=self.A_t(16 + hc, kc * 128, 128), s=self:
                           E.transpose(o, i, s.idb[:, :]))
                cps.append(lambda E,
                           o=self.VT[:, kc * 1024 + hc * 128:
                                     kc * 1024 + hc * 128 + 128],
                           i=self.pT(slot): E.tensor_copy(o, i))
            P.emit("t", tps, needs=[vkey, "idb", "f:wbuf0", "f:wbuf1",
                                    "fT%d" % bank, "fT%dv" % bank,
                                    "fT%ds" % bank],
                   inc="pe", produces=["trv"])
            P.emit("v", cps, needs=["trv"], inc="dve",
                   produces=["VT", "fT%d" % bank, "fT%dv" % bank,
                             "fT%ds" % bank])

    def _softmax(self, kn, scale, e_sb, attn_bf, needs_key, free_keys):
        P = self.P
        g = self.gsc
        P.emit("v", lambda E, s=self, kn=kn: E.reduce_max(
            g[:, 60:61], s.PS_A[:, 0:kn], axis=AX.X, negate=True),
            needs=[needs_key], inc="dve", produces=["nmax"])
        P.emit("v", lambda E, sc=scale, g=g: E.tensor_scalar(
            g[:, 61:62], g[:, 60:61], sc, None, ALU.mult),
            needs=["nmax"], inc="dve", produces=["biasS"])
        P.emit("s", lambda E, s=self, o=e_sb[:, 0:kn], kn=kn, sc=scale, g=g:
               E.activation(o, s.PS_A[:, 0:kn], AF.Exp, bias=g[:, 61:62], scale=sc),
               needs=["biasS"], inc="act", produces=["e"] + free_keys)
        P.emit("v", lambda E, i=e_sb[:, 0:kn], g=g: E.reduce_sum(
            g[:, 62:63], i, axis=AX.X),
            needs=["e"], inc="dve", produces=["ssum"])
        P.emit("v", lambda E, g=g: E.reciprocal(g[:, 63:64], g[:, 62:63]),
               needs=["ssum"], inc="dve", produces=["rsum"])
        P.emit("v", lambda E, o=attn_bf[:, 0:kn], i=e_sb[:, 0:kn], g=g:
               E.tensor_scalar(o, i, g[:, 63:64], None, ALU.mult),
               needs=["rsum"], inc="dve", produces=["attnq"])
        return "attnq"

    def p_attn_l0(self):
        P, S = self.P, self.S

        def raw(par):
            return self.B[:, par * S:par * S + S]

        def scl(par):
            return self.B[:, (2 + 2 * par) * S:(2 + 2 * par) * S + S]

        aT = self.B[:, 3 * S:4 * S]

        def emit_scores(i):
            qt = i
            par = i % 2
            g = self.gsc
            s0 = 16 + 16 * par
            for ns in range(self.NT4):
                pslice = self.PS_A[:, ns * 512:ns * 512 + 512]
                mms = []
                for dc in range(DC):
                    mms.append(lambda E, o=pslice,
                               l=self.A_t(dc, qt * 128, 128),
                               r=self.A_t(8 + dc, ns * 512, 512),
                               st=(dc == 0), sp=(dc == DC - 1):
                               E.matmul(o, l, r, start=st, stop=sp))
                P.emit("t", mms, needs=["q", "k", "f:pA%d" % ns],
                       inc="pe", produces=["sc%d.%d" % (par, ns)])
                if ns % 2 == 1 or ns == self.NT4 - 1:
                    m = ns // 2
                    w2 = min(1024, S - m * 1024)
                    mneeds = ["sc%d.%d" % (par, ns)]
                    if ns % 2 == 1:
                        mneeds.append("sc%d.%d" % (par, ns - 1))
                    P.emit("v", lambda E, g=g, o=g[:, s0 + m:s0 + m + 1],
                           i_=self.PS_A[:, m * 1024:m * 1024 + w2]:
                           E.reduce_max(o, i_, axis=AX.X),
                           needs=mneeds, inc="dve",
                           produces=["smx%d.%d" % (par, m)])

        def emit_softmax(i):
            par = i % 2
            g = self.gsc
            s0 = 16 + 16 * par
            nm = (self.NT4 + 1) // 2
            P.emit("v", lambda E, g=g, s0=s0, nm=nm: E.reduce_max(
                g[:, s0 + 8:s0 + 9], g[:, s0:s0 + nm], axis=AX.X, negate=True),
                needs=["smx%d.%d" % (par, m) for m in range(nm)],
                inc="dve", produces=["nmax%d" % par])
            for m in range(nm):
                w2 = min(1024, S - m * 1024)
                fr = ["f:pA%d" % (2 * m)]
                if w2 > 512:
                    fr.append("f:pA%d" % (2 * m + 1))
                P.emit("s", lambda E, s=self, m=m, w2=w2,
                       o=raw(par)[:, m * 1024:m * 1024 + w2], g=g, s0=s0:
                       E.activation(o, s.PS_A[:, m * 1024:m * 1024 + w2],
                                    AF.Exp, bias=g[:, s0 + 8:s0 + 9], scale=1.0,
                                    accum_out=g[:, s0 + 9 + m:s0 + 10 + m]),
                       needs=["nmax%d" % par] +
                             (["f:raw%d" % par] if m == 0 else []),
                       inc="act",
                       produces=["e%d.%d" % (par, m)] + fr)
            if nm > 1:
                P.emit("v", lambda E, g=g, s0=s0, nm=nm: E.reduce_sum(
                    g[:, s0 + 14:s0 + 15], g[:, s0 + 9:s0 + 9 + nm], axis=AX.X),
                    needs=["e%d.%d" % (par, m) for m in range(nm)],
                    inc="dve", produces=["ssum%d" % par])
                P.emit("v", lambda E, g=g, s0=s0: E.reciprocal(
                    g[:, s0 + 15:s0 + 16], g[:, s0 + 14:s0 + 15]),
                    needs=["ssum%d" % par], inc="dve", produces=["rsum%d" % par])
            else:
                P.emit("v", lambda E, g=g, s0=s0: E.reciprocal(
                    g[:, s0 + 15:s0 + 16], g[:, s0 + 9:s0 + 10]),
                    needs=["e%d.0" % par], inc="dve", produces=["rsum%d" % par])
            P.emit("v", lambda E, o=scl(par)[:, 0:S], i_=raw(par)[:, 0:S],
                   g=g, s0=s0:
                   E.tensor_scalar(o, i_, g[:, s0 + 15:s0 + 16], None, ALU.mult),
                   needs=["rsum%d" % par, "f:scl%d" % par], inc="dve",
                   produces=["attnq%d" % par, "f:raw%d" % par])

        def emit_tail(i):
            qt = i
            par = i % 2
            for g0 in range(0, self.NT, 8):
                gn = min(8, self.NT - g0)
                bank = (g0 // 8) % 2
                tps = []
                for i_ in range(gn):
                    kc = g0 + i_
                    slot = bank * 8 + i_
                    tps.append(lambda E, o=self.pT(slot),
                               i=scl(par)[:, kc * 128:(kc + 1) * 128], s=self:
                               E.transpose(o, i, s.idb[:, :]))
                P.emit("t", tps, needs=["attnq%d" % par, "idb",
                                        "fT%dv" % bank, "fT%ds" % bank],
                       inc="pe", produces=["trp"])
                use_v = (i + g0 // 8) % 2 == 0
                cps = []
                for i_ in range(gn):
                    kc = g0 + i_
                    slot = bank * 8 + i_
                    if use_v:
                        cps.append(lambda E, o=aT[:, kc * 128:(kc + 1) * 128],
                                   i=self.pT(slot): E.tensor_copy(o, i))
                    else:
                        cps.append(lambda E, o=aT[:, kc * 128:(kc + 1) * 128],
                                   i=self.pT(slot): E.copy(o, i))
                P.emit("v" if use_v else "s", cps, needs=["trp"],
                       inc="dve" if use_v else "act",
                       produces=["fT%dv" % bank, "fT%ds" % bank] +
                                ["aTc%d" % (g0 + i_) for i_ in range(gn)])
            for bank in range(2):
                mms = []
                for hc in range(4 * bank, 4 * bank + 4):
                    pdst = self.PS_B[:, hc * 128:(hc + 1) * 128]
                    for kc in range(self.NT):
                        mms.append(lambda E, o=pdst,
                                   l=self.VT[:, kc * 1024 + hc * 128:
                                             kc * 1024 + hc * 128 + 128],
                                   r=aT[:, kc * 128:(kc + 1) * 128],
                                   st=(kc == 0), sp=(kc == self.NT - 1):
                                   E.matmul(o, l, r, start=st, stop=sp))
                P.emit("t", mms,
                       needs=["VT", "f:pB%d" % bank] +
                             ["aTc%d" % kc for kc in range(self.NT)],
                       inc="pe", produces=["ctxp%d" % bank])
                octx = self.A[:, 0:24 * S].rearrange("p (blk s) -> p blk s",
                                                     blk=24)[
                    :, 16 + 4 * bank:16 + 4 * bank + 4,
                    qt * 128:(qt + 1) * 128]
                ictx = self.PS_B[:, 4 * bank * 128:
                                 (4 * bank + 4) * 128].rearrange(
                    "p (a b) -> p a b", a=4)
                P.emit("s", lambda E, o=octx, i=ictx: E.copy(o, i),
                       needs=["ctxp%d" % bank], inc="act",
                       produces=["ctx", "f:pB%d" % bank] +
                                (["f:scl%d" % par] if bank == 1 else []))

        n = self.NT
        for i in range(n + 1):
            if i < n:
                emit_scores(i)
            if i > 0:
                emit_tail(i - 1)
            if i < n:
                emit_softmax(i)
        P.alias("f:wbuf0", "ctxp1")
        P.alias("f:wbuf1", "ctxp1")
        P.alias("f:w1q0", "ctxp1")
        P.alias("f:w1q1", "ctxp1")
        P.alias("f:w2b", "ctxp1")

    def p_attn_l1(self):
        P, S = self.P, self.S
        units = [(j, qt) for qt in range(self.NT) for j in range(16)]

        def raw(par):
            return self.B[:, par * S:par * S + S]

        def scl(par):
            return self.B[:, (2 + 2 * par) * S:(2 + 2 * par) * S + S]

        aT = self.B[:, 3 * S:4 * S]

        def emit_scores(i):
            j, qt = units[i]
            dc = j // 2
            boff = 64 * (j % 2)
            kn = (qt + 1) * 128
            nsl = (kn + 511) // 512
            par = i % 2
            g = self.gsc
            s0 = 16 + 16 * par
            base = 0
            b0 = 0
            for ns in range(nsl):
                w = min(512, kn - ns * 512)
                pslice = self.PS_A[:, base + ns * 512:base + ns * 512 + w]
                mms = [lambda E, o=pslice,
                       l=self.A[boff:boff + 64,
                                dc * S + qt * 128: dc * S + qt * 128 + 128],
                       r=self.A[boff:boff + 64,
                                (8 + dc) * S + ns * 512:
                                (8 + dc) * S + ns * 512 + w],
                       sp=(ns != nsl - 1):
                       E.matmul(o, l, r, start=True, stop=sp)]
                if ns == nsl - 1:
                    mms.append(lambda E, s=self,
                               o=self.PS_A[:, base + qt * 128:
                                           base + qt * 128 + 128]:
                               E.matmul(o, s.idb[:, :], s.mask[:, :],
                                        start=False, stop=True))
                P.emit("t", mms,
                       needs=["q", "k", "f:pA%d" % (b0 + ns)] +
                             (["idb", "mask"] if ns == nsl - 1 else []),
                       inc="pe", produces=["sc%d.%d" % (par, ns)])
                if ns % 2 == 1 or ns == nsl - 1:
                    m = ns // 2
                    w2 = min(1024, kn - m * 1024)
                    mneeds = ["sc%d.%d" % (par, ns)]
                    if ns % 2 == 1:
                        mneeds.append("sc%d.%d" % (par, ns - 1))
                    P.emit("v", lambda E, g=g, o=g[:, s0 + m:s0 + m + 1],
                           i_=self.PS_A[:, base + m * 1024:
                                        base + m * 1024 + w2]:
                           E.reduce_max(o, i_, axis=AX.X),
                           needs=mneeds, inc="dve",
                           produces=["smx%d.%d" % (par, m)])

        def emit_softmax(i):
            j, qt = units[i]
            kn = (qt + 1) * 128
            nsl = (kn + 511) // 512
            par = i % 2
            g = self.gsc
            s0 = 16 + 16 * par
            nm = (nsl + 1) // 2
            P.emit("v", lambda E, g=g, s0=s0, nm=nm: E.reduce_max(
                g[:, s0 + 8:s0 + 9], g[:, s0:s0 + nm], axis=AX.X, negate=True),
                needs=["smx%d.%d" % (par, m) for m in range(nm)],
                inc="dve", produces=["nmax%d" % par])
            base = 0
            b0 = 0
            for m in range(nm):
                w2 = min(1024, kn - m * 1024)
                fr = ["f:pA%d" % (b0 + 2 * m)]
                if w2 > 512:
                    fr.append("f:pA%d" % (b0 + 2 * m + 1))
                P.emit("s", lambda E, s=self, m=m, w2=w2, base=base,
                       o=raw(par)[:, m * 1024:m * 1024 + w2], g=g, s0=s0:
                       E.activation(o, s.PS_A[:, base + m * 1024:
                                              base + m * 1024 + w2],
                                    AF.Exp, bias=g[:, s0 + 8:s0 + 9], scale=1.0,
                                    accum_out=g[:, s0 + 9 + m:s0 + 10 + m]),
                       needs=["nmax%d" % par] +
                             (["f:raw%d" % par] if m == 0 else []),
                       inc="act",
                       produces=["e%d.%d" % (par, m)] + fr)
            if nm > 1:
                P.emit("v", lambda E, g=g, s0=s0, nm=nm: E.reduce_sum(
                    g[:, s0 + 14:s0 + 15], g[:, s0 + 9:s0 + 9 + nm], axis=AX.X),
                    needs=["e%d.%d" % (par, m) for m in range(nm)],
                    inc="dve", produces=["ssum%d" % par])
                P.emit("v", lambda E, g=g, s0=s0: E.reciprocal(
                    g[:, s0 + 15:s0 + 16], g[:, s0 + 14:s0 + 15]),
                    needs=["ssum%d" % par], inc="dve", produces=["rsum%d" % par])
            else:
                P.emit("v", lambda E, g=g, s0=s0: E.reciprocal(
                    g[:, s0 + 15:s0 + 16], g[:, s0 + 9:s0 + 10]),
                    needs=["e%d.0" % par], inc="dve", produces=["rsum%d" % par])
            P.emit("v", lambda E, o=scl(par)[:, 0:kn], i_=raw(par)[:, 0:kn],
                   g=g, s0=s0:
                   E.tensor_scalar(o, i_, g[:, s0 + 15:s0 + 16], None, ALU.mult),
                   needs=["rsum%d" % par, "f:scl%d" % par], inc="dve",
                   produces=["attnq%d" % par, "f:raw%d" % par])

        ctx_cnt = 0

        def emit_tail(i):
            nonlocal ctx_cnt
            j, qt = units[i]
            dc = j // 2
            boff = 64 * (j % 2)
            par = i % 2
            for g0 in range(0, qt + 1, 8):
                gn = min(8, qt + 1 - g0)
                bank = (g0 // 8) % 2
                tps = []
                for i_ in range(gn):
                    kc = g0 + i_
                    slot = bank * 8 + i_
                    tps.append(lambda E, o=self.pT(slot),
                               i=scl(par)[:, kc * 128:(kc + 1) * 128], s=self:
                               E.transpose(o, i, s.idb[:, :]))
                P.emit("t", tps, needs=["attnq%d" % par, "idb",
                                        "fT%dv" % bank, "fT%ds" % bank],
                       inc="pe", produces=["trp"])
                use_v = (i + g0 // 8) % 2 == 0
                cps = []
                for i_ in range(gn):
                    kc = g0 + i_
                    slot = bank * 8 + i_
                    if use_v:
                        cps.append(lambda E, o=aT[:, kc * 128:(kc + 1) * 128],
                                   i=self.pT(slot): E.tensor_copy(o, i))
                    else:
                        cps.append(lambda E, o=aT[:, kc * 128:(kc + 1) * 128],
                                   i=self.pT(slot): E.copy(o, i))
                P.emit("v" if use_v else "s", cps, needs=["trp"],
                       inc="dve" if use_v else "act",
                       produces=["fT%dv" % bank, "fT%ds" % bank] +
                                ["aTc%d" % (g0 + i_) for i_ in range(gn)])
            cslot = ctx_cnt % 2
            ctx_cnt += 1
            pdst = self.PS_B[0:64, cslot * 512:cslot * 512 + 128]
            mms = []
            for kc in range(qt + 1):
                mms.append(lambda E, o=pdst,
                           l=self.VT[:, kc * 1024 + 64 * j:
                                     kc * 1024 + 64 * j + 64],
                           r=aT[:, kc * 128:(kc + 1) * 128],
                           st=(kc == 0), sp=(kc == qt):
                           E.matmul(o, l, r, start=st, stop=sp))
            P.emit("t", mms,
                   needs=["VT", "f:pB%d" % cslot] +
                         ["aTc%d" % kc for kc in range(qt + 1)],
                   inc="pe", produces=["ctxp"])
            P.emit("s", lambda E,
                   o=self.A[boff:boff + 64,
                            (16 + dc) * S + qt * 128:
                            (16 + dc) * S + qt * 128 + 128],
                   i=pdst: E.copy(o, i),
                   needs=["ctxp"], inc="act",
                   produces=["ctx", "f:pB%d" % cslot, "f:scl%d" % par])

        n = len(units)
        for i in range(n + 1):
            if i < n:
                emit_scores(i)
            if i > 0:
                emit_tail(i - 1)
            if i < n:
                emit_softmax(i)
        P.alias("f:pB0", "ctx")
        P.alias("f:pB1", "ctx")
        P.alias("f:wbuf0", "ctxp")
        P.alias("f:wbuf1", "ctxp")
        P.alias("f:w1q0", "ctxp")
        P.alias("f:w1q1", "ctxp")
        P.alias("f:w2b", "ctxp")

    def p_ln(self, out_ap=None):
        P, S, HS = self.P, self.S, self.HS
        mrow = self.sm[0:1, 0:HS]
        rrow = self.sm[0:1, HS:2 * HS]
        m2row = self.sm[0:1, 2 * HS:3 * HS]
        sq = self.sm[:, 3 * 1024:3 * 1024 + min(HS, 1024)]
        inv_d = 1.0 / 1024.0
        nsl = (HS + 511) // 512
        for th in range(2):
            t0 = th * HS
            lastk = "xs" if out_ap is None else "stgc"
            if th > 0:
                for b in range(4):
                    P.alias("f:pA%d" % b, lastk)
                for b in range(2):
                    P.alias("f:pB%d" % b, lastk)
            for dc in range(DC):
                P.emit("v", lambda E, o=sq[:, 0:HS],
                       a=self.sb_t(dc, t0, HS):
                       E.tensor_tensor(o, a, a, ALU.mult),
                       needs=["sb", "f:sq", "f:tmp"], inc="dve", produces=["sqd"])
                for ns in range(nsl):
                    w = min(512, HS - ns * 512)
                    P.emit("t", lambda E,
                           o=self.PS_A[0:1, ns * 512:ns * 512 + w],
                           l=self.ones[:, 0:1],
                           r=self.sb_t(dc, t0 + ns * 512, w),
                           st=(dc == 0), sp=(dc == DC - 1):
                           E.matmul(o, l, r, start=st, stop=sp),
                           needs=(["sb", "ones", "f:pA0", "f:pA1"]
                                  if (dc == 0 and ns == 0) else ["sb"]),
                           inc="pe" if dc == DC - 1 else None,
                           produces=["psum"] if dc == DC - 1 else [])
                for ns in range(nsl):
                    w = min(512, HS - ns * 512)
                    P.emit("t", lambda E,
                           o=self.PS_A[0:1, 1024 + ns * 512: 1024 + ns * 512 + w],
                           l=self.ones[:, 0:1], r=sq[:, ns * 512:ns * 512 + w],
                           st=(dc == 0), sp=(dc == DC - 1):
                           E.matmul(o, l, r, start=st, stop=sp),
                           needs=(["sqd", "f:pA2", "f:pA3"]
                                  if (dc == 0 and ns == 0) else ["sqd"]),
                           inc="pe", produces=["sqrd"] + (["psq"] if dc == DC - 1 else []))
                P.alias("f:sq", "sqrd")
            P.emit("v", lambda E, o=mrow, s=self, iv=inv_d: E.tensor_scalar(
                o, s.PS_A[0:1, 0:HS], iv, None, ALU.mult),
                needs=["psum"], inc="dve", produces=["mrow", "f:pA0", "f:pA1"])
            P.emit("v", lambda E, s=self, iv=inv_d: E.tensor_scalar(
                s.PS_A[0:1, 1024:1024 + HS], s.PS_A[0:1, 1024:1024 + HS], iv, None, ALU.mult),
                needs=["psq"], inc="dve", produces=["msq"])
            P.emit("v", lambda E, o=m2row, i=mrow: E.tensor_tensor(o, i, i, ALU.mult),
                   needs=["mrow"], inc="dve", produces=["m2"])
            P.emit("v", lambda E, s=self, i=m2row: E.tensor_tensor(
                s.PS_A[0:1, 1024:1024 + HS], s.PS_A[0:1, 1024:1024 + HS], i, ALU.subtract),
                needs=["msq", "m2"], inc="dve", produces=["var"])
            P.emit("v", lambda E, s=self: E.tensor_scalar(
                s.PS_A[0:1, 1024:1024 + HS], s.PS_A[0:1, 1024:1024 + HS], 1e-5, None, ALU.add),
                needs=["var"], inc="dve", produces=["vare"])
            P.emit("s", lambda E, s=self: E.activation(
                s.PS_A[0:1, 1024:1024 + HS], s.PS_A[0:1, 1024:1024 + HS], AF.Ln),
                needs=["vare"], inc="act", produces=["lnv"])
            P.emit("s", lambda E, s=self, o=rrow: E.activation(
                o, s.PS_A[0:1, 1024:1024 + HS], AF.Exp, scale=-0.5),
                needs=["lnv"], inc="act", produces=["rrow", "f:pA2", "f:pA3"])
            for ns in range(nsl):
                w = min(512, HS - ns * 512)
                P.emit("t", lambda E, o=self.PS_B[:, ns * 512:ns * 512 + w],
                       l=self.ones[0:1, 0:128],
                       r=mrow[0:1, ns * 512:ns * 512 + w]:
                       E.matmul(o, l, r, start=True, stop=True),
                       needs=["mrow", "ones", "f:pB%d" % ns],
                       inc="pe", produces=["mbc%d" % ns])
                P.emit("t", lambda E, o=self.PS_A[:, ns * 512:ns * 512 + w],
                       l=self.ones[0:1, 0:128],
                       r=rrow[0:1, ns * 512:ns * 512 + w]:
                       E.matmul(o, l, r, start=True, stop=True),
                       needs=["rrow", "ones", "f:pA%d" % ns],
                       inc="pe", produces=["rbc%d" % ns])
            if out_ap is None:
                tmp = self.sm[:, 3 * 1024:3 * 1024 + 512]
                for dc in range(DC):
                    for nb in range(nsl):
                        w = min(512, HS - nb * 512)
                        P.emit("v", lambda E, o=tmp[:, 0:w],
                               a=self.sb_t(dc, t0 + nb * 512, w),
                               m=self.PS_B[:, nb * 512:nb * 512 + w]:
                               E.tensor_tensor(o, a, m, ALU.subtract),
                               needs=["mbc%d" % nb, "sb", "f:tmp", "sqrd"],
                               inc="dve", produces=["tmpd"])
                        P.emit("v", lambda E,
                               o=self.xs_t(dc, t0 + nb * 512, w),
                               a=tmp[:, 0:w],
                               r=self.PS_A[:, nb * 512:nb * 512 + w]:
                               E.tensor_tensor(o, a, r, ALU.mult),
                               needs=["rbc%d" % nb, "tmpd"],
                               inc="dve", produces=["xs", "f:tmp"])
            else:
                for tcl in range(HS // 128):
                    tc = th * (HS // 128) + tcl
                    roomy = (DC * self.S // 2) >= 3072
                    sp = (tc % 2) if roomy else 0
                    stage = self.xs_f32[:, sp * 1024:sp * 1024 + 1024]
                    fb = 2048 if roomy else 1024
                    fxy8 = self.xs_f32[:, fb:fb + 1024]
                    nb = (tcl * 128) // 512
                    ci = tcl * 128 - nb * 512
                    subs, mults = [], []
                    for dc in range(DC):
                        fx = fxy8[:, dc * 128:(dc + 1) * 128]
                        subs.append(lambda E, o=fx,
                                    a=self.sb_t(dc, t0 + tcl * 128, 128),
                                    m=self.PS_B[:, nb * 512 + ci:
                                                nb * 512 + ci + 128]:
                                    E.tensor_tensor(o, a, m, ALU.subtract))
                        mults.append(lambda E, o=fx, a=fx,
                                     r=self.PS_A[:, nb * 512 + ci:
                                                 nb * 512 + ci + 128]:
                                     E.tensor_tensor(o, a, r, ALU.mult))
                    P.emit("v", subs, needs=["mbc%d" % nb, "sb", "f:fx"],
                           inc="dve", produces=["fxs"])
                    P.emit("v", mults, needs=["rbc%d" % nb, "fxs"],
                           inc="dve", produces=["fxm"])
                    tps = []
                    for dc in range(DC):
                        fx = fxy8[:, dc * 128:(dc + 1) * 128]
                        pdst = self.PS_A[:, 1024 + dc * 128:1024 + dc * 128 + 128]
                        tps.append(lambda E, o=pdst, i=fx, s=self:
                                   E.transpose(o, i, s.idf[:, :]))
                    P.emit("t", tps, needs=["fxm", "idf", "f:ot"],
                           inc="pe", produces=["otr"])
                    cps = []
                    for dc in range(DC):
                        pdst = self.PS_A[:, 1024 + dc * 128:1024 + dc * 128 + 128]
                        cps.append(lambda E, o=stage[:, dc * 128:(dc + 1) * 128],
                                   i=pdst: E.tensor_copy(o, i))
                    P.emit("v", cps, needs=["otr", "f:stgo%d" % sp], inc="dve",
                           produces=["f:ot", "stgc", "f:fx"])
                    if sp == 0:
                        P.emit("y", lambda E, o=out_ap[tc * 128:(tc + 1) * 128, :],
                               i=stage: E.dma_start(o, i),
                               needs=["stgc"], inc="hw",
                               produces=["f:stgo%d" % sp], amt=16)
                    else:
                        P.emit("g", lambda E, o=out_ap[tc * 128:(tc + 1) * 128, :],
                               i=stage: E.dma_start(o, i),
                               needs=["stgc"], inc="ld",
                               produces=["f:stgo%d" % sp], amt=16)
        P.alias("f:sm_rows", "rbc%d" % (nsl - 1))
        lastk = "xs" if out_ap is None else "stgc"
        for b in range(4):
            P.alias("f:pA%d" % b, lastk)
        for b in range(2):
            P.alias("f:pB%d" % b, lastk)
        if out_ap is None:
            P.alias("f:sm_tmp", "xs")
        else:
            P.alias("f:sm_tmp", "rbc%d" % (nsl - 1))
            P._wait("y", "hw", P.cnt["hw"])
            P._wait("g", "ld", P.cnt["ld"])

    def p_gate(self, l1, pfx):
        P, S = self.P, self.S
        g = self.gsc
        if not l1:
            rw = self.dram_in("l0_rwT", [D, 8])
            rb = self.dram_in("l0_rb", [1, 8])
            lw = self.dram_in("l0_lwT", [D, 4])
            lb = self.dram_in("l0_lb", [1, 4])
            P.emit("g", lambda E, s=self,
                   a=rw.rearrange("(dc p) e -> p dc e", p=128):
                   E.dma_start(s.rwsb[:, :].rearrange("p (dc e) -> p dc e", dc=8), a), inc="ld", produces=["rw"], amt=16)
            P.emit("g", lambda E, s=self,
                   a=lw.rearrange("(dc p) e -> p dc e", p=128):
                   E.dma_start(s.lwsb[:, :].rearrange("p (dc e) -> p dc e", dc=8), a), inc="ld", produces=["lwk"], amt=16)
            P.emit("g", lambda E, s=self, a=rb: E.dma_start(s.brow[:, :], a),
                   inc="ld", produces=["brow"], amt=16)
            P.emit("g", lambda E, s=self, a=lb: E.dma_start(s.lrow[:, :], a),
                   inc="ld", produces=["lrow"], amt=16)
            wsb, wk, brow_sb, bk, wc = self.rwsb, "rw", self.brow, "brow", 8
        else:
            wgap = self.dram_in("l1_wg", [D, 8])
            bg = self.dram_in("l1_bg", [1, 8])
            P.emit("g", lambda E, s=self,
                   a=wgap.rearrange("(dc p) e -> p dc e", p=128):
                   E.dma_start(s.wgsb[:, :].rearrange("p (dc e) -> p dc e", dc=8), a), inc="ld", produces=["wgk"], amt=16)
            P.emit("g", lambda E, s=self, a=bg: E.dma_start(s.bgrow[:, :], a),
                   inc="ld", produces=["bgrow"], amt=16)
            wsb, wk, brow_sb, bk, wc = self.wgsb, "wgk", self.bgrow, "bgrow", 8

        def ch(eng, fn, needs=(), produces=()):
            sem = {"v": "dve", "s": "act"}[eng]
            P.emit(eng, fn, needs=list(needs) + ["ch"], inc=sem,
                   produces=list(produces) + ["ch"])

        for tt in range(self.NT):
            mms = []
            for dc in range(DC):
                mms.append(lambda E, o=self.PS_A[:, 0:8],
                           l=self.xs_t(dc, tt * 128, 128),
                           r=wsb[:, dc * wc:dc * wc + 8],
                           st=(dc == 0): E.matmul(o, l, r, start=st, stop=False))
            mms.append(lambda E, o=self.PS_A[:, 0:8], s=self, br=brow_sb:
                       E.matmul(o, s.ones[0:1, 0:128], br[0:1, 0:8],
                                start=False, stop=True))
            P.emit("t", mms, needs=["xs", wk, bk, "ones", "f:pA0"],
                   inc="pe", produces=["rlp"])
            if not l1:
                mms = []
                for dc in range(DC):
                    mms.append(lambda E, o=self.PS_A[:, 8:12],
                               l=self.xs_t(dc, tt * 128, 128),
                               r=self.lwsb[:, dc * 4:dc * 4 + 4],
                               st=(dc == 0): E.matmul(o, l, r, start=st, stop=False))
                mms.append(lambda E, o=self.PS_A[:, 8:12], s=self:
                           E.matmul(o, s.ones[0:1, 0:128], s.lrow[0:1, 0:4],
                                    start=False, stop=True))
                P.emit("t", mms, needs=["xs", "lwk", "lrow", "ones"],
                       inc="pe", produces=["lgp"])
            if "ch" not in P.last:
                P.last["ch"] = P.last["rlp"]
            if not l1:
                ch("v", lambda E, s=self, g=g: E.tensor_copy(g[:, 0:4], s.PS_A[:, 8:12]),
                   needs=["lgp"])
                ch("v", lambda E, g=g: E.reduce_max(g[:, 60:61], g[:, 0:4],
                                                    axis=AX.X, negate=True))
                ch("s", lambda E, g=g: E.activation(g[:, 4:8], g[:, 0:4], AF.Exp,
                                                    bias=g[:, 60:61], scale=1.0))
                ch("v", lambda E, g=g: E.reduce_sum(g[:, 61:62], g[:, 4:8], axis=AX.X))
                ch("v", lambda E, g=g: E.reciprocal(g[:, 62:63], g[:, 61:62]))
                ch("v", lambda E, g=g: E.tensor_scalar(g[:, 8:12], g[:, 4:8],
                                                       g[:, 62:63], 0.1,
                                                       ALU.mult, ALU.mult))
                ch("v", lambda E, g=g: E.tensor_copy(g[:, 12:16], g[:, 8:12]))
                ch("v", lambda E, g=g: E.tensor_copy(g[:, 16:20], g[:, 8:12]))
                ch("v", lambda E, g=g: E.tensor_tensor(g[:, 12:15], g[:, 12:15],
                                                       g[:, 9:12], ALU.add))
                ch("v", lambda E, g=g: E.tensor_tensor(g[:, 16:19], g[:, 16:19],
                                                       g[:, 9:12], ALU.add))
                ch("v", lambda E, s=self, g=g: E.tensor_tensor(
                    g[:, 20:28], s.PS_A[:, 0:8], g[:, 12:20], ALU.add),
                   needs=["rlp"], produces=["f:pA0"])
            else:
                ch("v", lambda E, s=self, g=g: E.tensor_copy(g[:, 20:28], s.PS_A[:, 0:8]),
                   needs=["rlp"], produces=["f:pA0"])
            ch("v", lambda E, g=g: E.reduce_max(g[:, 60:61], g[:, 20:28],
                                                axis=AX.X, negate=True))
            ch("s", lambda E, g=g: E.activation(g[:, 28:36], g[:, 20:28], AF.Exp,
                                                bias=g[:, 60:61], scale=1.0))
            ch("v", lambda E, g=g: E.reduce_sum(g[:, 61:62], g[:, 28:36], axis=AX.X))
            ch("v", lambda E, g=g: E.reciprocal(g[:, 62:63], g[:, 61:62]))
            ch("v", lambda E, g=g: E.tensor_scalar(g[:, 36:44], g[:, 28:36],
                                                   g[:, 62:63], None, ALU.mult))
            if not l1:
                ch("v", lambda E, g=g: E.tensor_scalar(g[:, 44:52], g[:, 20:28],
                                                       g[:, 60:61], 0.0,
                                                       ALU.add, ALU.is_ge))
                ch("v", lambda E, g=g: E.tensor_scalar(g[:, 44:52], g[:, 44:52],
                                                       -30000.0, None, ALU.mult))
                ch("v", lambda E, g=g: E.tensor_tensor(g[:, 44:52], g[:, 44:52],
                                                       g[:, 20:28], ALU.add))
                ch("v", lambda E, g=g: E.reduce_max(g[:, 59:60], g[:, 44:52],
                                                    axis=AX.X, negate=True))
                ch("v", lambda E, g=g: E.tensor_scalar(g[:, 44:52], g[:, 20:28],
                                                       g[:, 59:60], 0.0,
                                                       ALU.add, ALU.is_ge))
                ch("v", lambda E, g=g: E.tensor_tensor(g[:, 52:60], g[:, 36:44],
                                                       g[:, 44:52], ALU.mult))
                ch("v", lambda E, g=g: E.reduce_sum(g[:, 61:62], g[:, 52:60], axis=AX.X))
                ch("v", lambda E, g=g: E.reciprocal(g[:, 62:63], g[:, 61:62]))
                ch("v", lambda E, g=g: E.tensor_scalar(g[:, 52:60], g[:, 52:60],
                                                       g[:, 62:63], None, ALU.mult))
            else:
                ch("v", lambda E, g=g: E.tensor_scalar(g[:, 44:52], g[:, 20:28],
                                                       g[:, 60:61], 0.0,
                                                       ALU.add, ALU.is_ge))
                ch("v", lambda E, g=g: E.tensor_tensor(g[:, 52:60], g[:, 36:44],
                                                       g[:, 44:52], ALU.mult))
            ch("v", lambda E, s=self, g=g: E.tensor_copy(s.gtb[:, :], g[:, 52:60]),
               produces=["gtbd"])
            P.emit("t", lambda E, s=self: E.transpose(
                s.PS_T[0:8, 0:128], s.gtb[:, :], s.idb[:, :]),
                needs=["gtbd", "idb", "fT0v", "fT0s"], inc="pe",
                produces=["gtr"])
            P.emit("v", lambda E, s=self, c0=tt * 128: E.tensor_copy(
                s.gfm_t(c0, 128), s.PS_T[0:8, 0:128]),
                needs=["gtr", "psum", "psq", "xs"], inc="dve",
                produces=["gfm", "fT0", "fT0v", "fT0s", "ch"])

    def p_ffn(self, l1, pfx):
        P, S = self.P, self.S
        if not l1:
            w1_ap = self.dram_in("l0_w1", [8, D, 4096])
            b1_ap = self.dram_in("l0_b1", [8, 4096])
            w2_ap = self.dram_in("l0_w2", [8, 4096, D])
            b2_ap = self.dram_in("l0_b2", [8, D])
            HF, nhalf = 4096, 2
            b1off = 0
        else:
            w1_ap = self.dram_in("l1_w1", [8, D, D])
            b1_ap = self.dram_in("l1_b1", [8, D])
            w2_ap = self.dram_in("l1_w2", [8, D, D])
            b2_ap = self.dram_in("l1_b2", [8, D])
            HF, nhalf = 1024, 1
            b1off = 256
        hfc = HF // 128 // nhalf
        hw = HF // nhalf
        P.emit("g", lambda E, s=self, a=b2_ap: E.dma_start(s.b2s[:, :], a),
               needs=["f:b2s"], inc="ld", produces=["b2s"], amt=16)
        b1sb = self.sm[0:8, 0:HF]
        P.emit("g", lambda E, o=b1sb, a=b1_ap: E.dma_start(o, a),
               needs=["f:b1sb", "f:stg0", "f:stg1", "f:sm_rows", "f:sm_tmp"],
               inc="ld", produces=["b1sb"], amt=16)
        for c in range(HF // 128):
            P.emit("t", lambda E, s=self, i=b1sb[0:8, c * 128:(c + 1) * 128]:
                   E.transpose(s.PS_T[0:128, 0:8], i, s.idb[0:8, 0:8]),
                   needs=["b1sb", "idb", "fT0v", "fT0s"], inc="pe",
                   produces=["b1tr"])
            P.emit("v", lambda E, s=self, c0=b1off + c * 8: E.tensor_copy(
                s.b1T[:, c0:c0 + 8], s.PS_T[0:128, 0:8]),
                needs=["b1tr"], inc="dve",
                produces=["b1T", "fT0", "fT0v", "fT0s"])
        ACC = self.A_f32
        for oc in range(DC):
            for nt in range(self.NT4):
                pk = "pB%d" % (nt % 2)
                pslice = self.PS_B[:, (nt % 2) * 512:(nt % 2) * 512 + 512]
                P.emit("t", lambda E, o=pslice,
                       l=self.b2s[0:8, oc * 128:(oc + 1) * 128],
                       r=self.gfm_t(nt * 512, 512):
                       E.matmul(o, l, r, start=True, stop=True),
                       needs=["b2s", "gfm", "f:" + pk], inc="pe",
                       produces=["b2m"])
                P.emit("v", lambda E,
                       o=ACC[:, oc * S + nt * 512: oc * S + nt * 512 + 512],
                       i=pslice: E.tensor_copy(o, i),
                       needs=["b2m"], inc="dve",
                       produces=["AC%d.%d" % (oc, nt), "f:" + pk])
        g_bc = self.B_f32[:, 4096:4096 + S]
        tmpb = 16 * 512 + 2 * S
        for e in range(8):
            for nt in range(self.NT4):
                P.emit("t", lambda E, s=self, o=self.PS_A[:, 0:512],
                       l=self.sel[0:8, e * 128:(e + 1) * 128],
                       r=self.gfm_t(nt * 512, 512):
                       E.matmul(o, l, r, start=True, stop=True),
                       needs=["sel", "gfm", "f:pA0"], inc="pe", produces=["gbp"])
                P.emit("v", lambda E, o=g_bc[:, nt * 512:nt * 512 + 512],
                       i=self.PS_A[:, 0:512]: E.tensor_copy(o, i),
                       needs=["gbp", "gbcrd", "xs"], inc="dve",
                       produces=["gbc", "f:pA0"])
            for half in range(nhalf):
                hq = hw // 2
                for q in range(2):
                    w1src = w1_ap[e, :, half * hw + q * hq:
                                  half * hw + q * hq + hq].rearrange(
                        "(dc p) h -> p dc h", p=128)
                    w1dst = self.A[:, 2 * DC * S: 2 * DC * S + DC * hw].rearrange(
                        "p (dc h) -> p dc h", dc=DC)[:, :, q * hq:(q + 1) * hq]
                    P.emit("g", lambda E, o=w1dst, a=w1src: E.dma_start(o, a),
                           needs=["f:w1q%d" % q], inc="ldw",
                           produces=["w1q%d" % q], amt=16)
                w2src = w2_ap[e, half * hw:half * hw + hw, :].rearrange(
                    "(hc p) d -> p hc d", p=128)
                w2dst = self.VT[:, 0:hfc * 1024].rearrange(
                    "p (hc d) -> p hc d", hc=hfc)
                P.emit("g", lambda E, o=w2dst, a=w2src: E.dma_start(o, a),
                       needs=["f:w2b", "f:wbuf0", "f:wbuf1"],
                       inc="ldw2", produces=["w2b"], amt=16)
                for tt in range(self.NT4):
                    for hc in range(hfc):
                        pk = "pA%d" % (hc % 2)
                        pslice = self.PS_A[:, (hc % 2) * 512:(hc % 2) * 512 + 512]
                        mms = []
                        for dc in range(DC):
                            mms.append(lambda E, o=pslice,
                                       l=self.A[:, 2 * DC * S + dc * hw + hc * 128:
                                                2 * DC * S + dc * hw + hc * 128 + 128],
                                       r=self.xs_t(dc, tt * 512, 512),
                                       st=(dc == 0), sp=(dc == DC - 1):
                                       E.matmul(o, l, r, start=st, stop=sp))
                        qq = 0 if hc < hfc // 2 else 1
                        prod = ["hp%d" % (hc % 2)]
                        if tt == self.NT4 - 1 and hc == hfc // 2 - 1:
                            prod.append("w1q0rd")
                        if tt == self.NT4 - 1 and hc == hfc - 1:
                            prod.append("w1q1rd")
                        P.emit("t", mms, needs=["w1q%d" % qq, "xs", "f:" + pk],
                               inc="pe", produces=prod)
                        hcg = half * hfc + hc
                        tmp = self.B[:, tmpb + (hc % 2) * 512:
                                     tmpb + (hc % 2) * 512 + 512]
                        P.emit("s", lambda E, o=tmp, i=pslice, s=self,
                               bc=b1off + hcg * 8 + e:
                               E.activation(o, i, AF.Gelu,
                                            bias=s.b1T[:, bc:bc + 1], scale=1.0),
                               needs=["hp%d" % (hc % 2), "b1T",
                                      "f:tm%d" % (hc % 2)],
                               inc="act", produces=["ge%d" % (hc % 2), "f:" + pk])
                        P.emit("v", lambda E, o=self.B[:, hc * 512:(hc + 1) * 512],
                               i=tmp, gb=g_bc[:, tt * 512:tt * 512 + 512]:
                               E.tensor_tensor(o, i, gb, ALU.mult),
                               needs=["ge%d" % (hc % 2), "gbc", "f:hsb"],
                               inc="dve",
                               produces=["h%d" % hc, "f:tm%d" % (hc % 2)])
                    for oc in range(DC):
                        pk = "pB%d" % (oc % 2)
                        pslice = self.PS_B[:, (oc % 2) * 512:(oc % 2) * 512 + 512]
                        mms = []
                        for hc in range(hfc):
                            mms.append(lambda E, o=pslice,
                                       l=self.VT[:, hc * 1024 + oc * 128:
                                                 hc * 1024 + oc * 128 + 128],
                                       r=self.B[:, hc * 512:(hc + 1) * 512],
                                       st=(hc == 0), sp=(hc == hfc - 1):
                                       E.matmul(o, l, r, start=st, stop=sp))
                        P.emit("t", mms,
                               needs=["w2b", "f:" + pk] +
                                     ["h%d" % hc for hc in range(hfc)],
                               inc="pe", produces=["op"])
                        P.emit("v", lambda E,
                               o=ACC[:, oc * S + tt * 512: oc * S + tt * 512 + 512],
                               i=pslice: E.tensor_tensor(o, i, o, ALU.add),
                               needs=["op", "AC%d.%d" % (oc, tt)], inc="dve",
                               produces=["AC%d.%d" % (oc, tt), "f:" + pk])
                    P.alias("f:hsb", "op")
                P.alias("f:w1q0", "w1q0rd")
                P.alias("f:w1q1", "w1q1rd")
                P.alias("f:w2b", "op")
            P.alias("gbcrd", "op")
        for dc in range(DC):
            for nt in range(self.NT4):
                P.emit("v", lambda E, o=self.sb_t(dc, nt * 512, 512),
                       a=ACC[:, dc * S + nt * 512: dc * S + nt * 512 + 512],
                       x=self.xs_t(dc, nt * 512, 512):
                       E.tensor_tensor(o, a, x, ALU.add),
                       needs=["AC%d.%d" % (dc, nt), "xs"], inc="dve",
                       produces=["sb"])
        P.alias("f:b2s", "b2m")
        P.alias("f:b1sb", "b1tr")
        P.alias("f:wbuf0", "op")
        P.alias("f:wbuf1", "op")


    def emit_l0a(self):
        P = self.P
        WqT = self.dram_in("l0_WqT", [D, D])
        WkT = self.dram_in("l0_WkT", [D, D])
        WvT = self.dram_in("l0_WvT", [D, D])
        WoT = self.dram_in("l0_WoT", [D, D])
        xsrc = lambda dc, c0, w: self.xs_t(dc, c0, w)
        for sl, (tag, wap, b0, dk) in enumerate((("0q", WqT, 0, "q"),
                                                 ("0k", WkT, 8, "k"),
                                                 ("0v", WvT, 16, "v"))):
            self.p_proj(wap, tag, xsrc, "xs",
                        dst_t=(lambda oc, c0, w, b=b0: self.A_t(b + oc, c0, w)),
                        dst_key=dk, slot=sl % 2,
                        out_scale=(1.0 / 32.0 if dk == "q" else None))
        self.p_build_vt("v")
        self.p_attn_l0()
        self.p_proj(WoT, "0o", (lambda dc, c0, w: self.A_t(16 + dc, c0, w)),
                    "ctx", resid=True, slot=1)

    def emit_l1a(self):
        P = self.P
        Wa = self.dram_in("l1_Wa", [D, 3 * D])
        ba = self.dram_in("l1_ba", [24, 128])
        Wp = self.dram_in("l1_Wp", [D, D])
        bp = self.dram_in("l1_bp", [8, 128])
        stg = self.sm[0:24, 0:128]
        P.emit("g", lambda E, o=stg, a=ba: E.dma_start(o, a),
               needs=["f:b1sb", "f:stg0", "f:stg1", "f:sm_rows", "f:sm_tmp"],
               inc="ld", produces=["bast"], amt=16)
        P.emit("t", lambda E, s=self, i=stg: E.transpose(
            s.PS_T[0:128, 0:24], i, s.idb[0:24, 0:24]),
            needs=["bast", "idb", "fT0v", "fT0s"], inc="pe",
            produces=["batr"])
        P.emit("v", lambda E, s=self: E.tensor_copy(
            s.baT[:, 0:24], s.PS_T[0:128, 0:24]),
            needs=["batr"], inc="dve",
            produces=["baT", "fT0", "fT0v", "fT0s"])
        stg2 = self.sm[0:8, 128:256]
        P.emit("g", lambda E, o=stg2, a=bp: E.dma_start(o, a),
               needs=["f:b1sb", "f:stg0", "f:stg1", "f:sm_rows", "f:sm_tmp"],
               inc="ld", produces=["bpst"], amt=16)
        P.emit("t", lambda E, s=self, i=stg2: E.transpose(
            s.PS_T[0:128, 0:8], i, s.idb[0:8, 0:8]),
            needs=["bpst", "idb", "fT0v", "fT0s"], inc="pe",
            produces=["bptr"])
        P.emit("v", lambda E, s=self: E.tensor_copy(
            s.baT[:, 24:32], s.PS_T[0:128, 0:8]),
            needs=["bptr"], inc="dve",
            produces=["bpT", "fT0", "fT0v", "fT0s"])
        P.alias("f:b1sb", "bptr")
        xsrc = lambda dc, c0, w: self.xs_t(dc, c0, w)
        for sec, (tag, b0, dk) in enumerate((("1q", 0, "q"), ("1k", 8, "k"),
                                             ("1v", 16, "v"))):
            wsec = Wa[:, sec * D:(sec + 1) * D]
            self.p_proj(wsec, tag, xsrc, "xs",
                        dst_t=(lambda oc, c0, w, b=b0: self.A_t(b + oc, c0, w)),
                        dst_key=dk,
                        bias_col=(lambda oc, s0=sec * 8: self.baT[:, s0 + oc:s0 + oc + 1]),
                        bias_key="baT", slot=sec % 2,
                        out_scale=(0.125 if dk == "q" else None))
        self.p_build_vt("v")
        self.p_attn_l1()
        self.p_proj(Wp, "1o", (lambda dc, c0, w: self.A_t(16 + dc, c0, w)),
                    "ctx", resid=True,
                    bias_col=(lambda oc: self.baT[:, 24 + oc:24 + oc + 1]),
                    bias_key="bpT", slot=1)

    def finalize(self):
        P = self.P
        nc = self.nc
        with nc.Block() as block:
            @block.tensor
            def _(E):
                for f in P.ops["t"]:
                    f(E)

            @block.vector
            def _(E):
                for f in P.ops["v"]:
                    f(E)

            @block.scalar
            def _(E):
                for f in P.ops["s"]:
                    f(E)

            @block.gpsimd
            def _(E):
                for f in P.ops["g"]:
                    f(E)

            @block.sync
            def _(E):
                for f in P.ops["y"]:
                    f(E)
        self.es.close()
        return nc


def build_program(parts, S):
    kb = KB(S)
    kb.alloc()
    need_mask = "l1a" in parts
    need_sel = ("l0f" in parts) or ("l1f" in parts)
    kb.p_consts(need_mask, need_sel, need_idf=True)
    kb.p_load_x()
    out_ap = kb.dram_out("x_out", [S, D])
    for i, part in enumerate(parts):
        last = (i == len(parts) - 1)
        if part == "l0a":
            kb.emit_l0a()
        elif part == "l0f":
            kb.p_gate(False, "0")
            kb.p_ffn(False, "0")
        elif part == "l1a":
            kb.emit_l1a()
        elif part == "l1f":
            kb.p_gate(True, "1")
            kb.p_ffn(True, "1")
        kb.p_ln(out_ap=out_ap if last else None)
    return kb



def _prep(inputs):
    f = lambda a: np.ascontiguousarray(a, dtype=np.float32)
    w = {}
    W_qkv = inputs["L0_attn_W_qkv"]
    w["l0_WqT"] = f(W_qkv[0::3, :].T)
    w["l0_WkT"] = f(W_qkv[1::3, :].T)
    w["l0_WvT"] = f(W_qkv[2::3, :].T)
    w["l0_WoT"] = f(inputs["L0_attn_W_out"].T)
    w["l0_rwT"] = f(inputs["L0_ffn_router_w"].T)
    w["l0_rb"] = f(inputs["L0_ffn_router_b"].reshape(1, 8))
    w["l0_lwT"] = f(inputs["L0_ffn_lang_w"].T)
    w["l0_lb"] = f(inputs["L0_ffn_lang_b"].reshape(1, 4))
    w["l0_w1"] = f(inputs["L0_ffn_w1"])
    w["l0_b1"] = f(inputs["L0_ffn_b1"])
    w["l0_w2"] = f(inputs["L0_ffn_w2"])
    w["l0_b2"] = f(inputs["L0_ffn_b2"])
    w["l1_Wa"] = f(inputs["L1_attn_W_attn"])
    w["l1_ba"] = f(inputs["L1_attn_b_attn"].reshape(24, 128))
    w["l1_Wp"] = f(inputs["L1_attn_W_proj"])
    w["l1_bp"] = f(inputs["L1_attn_b_proj"].reshape(8, 128))
    w["l1_wg"] = f(inputs["L1_ffn_Wg"])
    w["l1_bg"] = f(inputs["L1_ffn_bg"].reshape(1, 8))
    w["l1_w1"] = f(inputs["L1_ffn_W1"])
    w["l1_b1"] = f(inputs["L1_ffn_b1"])
    w["l1_w2"] = f(inputs["L1_ffn_W2"])
    w["l1_b2"] = f(inputs["L1_ffn_b2"])
    ii, jj = np.meshgrid(np.arange(128), np.arange(128), indexing="ij")
    w["c_mask"] = np.where(jj <= ii, 0.0, -30000.0).astype(np.float32)
    w["c_ident"] = np.eye(128, dtype=np.float32)
    sel = np.zeros((8, 1024), dtype=np.float32)
    for e in range(8):
        sel[e, e * 128:(e + 1) * 128] = 1.0
    w["c_sel"] = sel
    return w


_STAGE_PARTS = {
    "staged4": [["l0a"], ["l0f"], ["l1a"], ["l1f"]],
    "fused": [["l0a", "l0f", "l1a", "l1f"]],
}


def kernel(**inputs):
    w = _prep(inputs)
    x = np.ascontiguousarray(inputs["L0_attn_x"], dtype=np.float32)
    B = x.shape[0]
    xcur = [np.ascontiguousarray(x[c]) for c in range(B)]
    for parts in _STAGE_PARTS[MODE]:
        kb = build_program(parts, S_FULL)
        nc = kb.finalize()
        names = set(kb.inp.keys())
        in_maps = []
        for c in range(B):
            m = {k: w[k] for k in names if k in w}
            if "x_in" in names:
                m["x_in"] = xcur[c]
            in_maps.append(m)
        res = run_bass_kernel_spmd(nc, in_maps, core_ids=list(range(NCORES)))
        xcur = [np.ascontiguousarray(res.results[c]["x_out"]) for c in range(B)]
    return np.stack(xcur, axis=0)


# version 20 (final)
# speedup vs baseline: 1.1793x; 1.1793x over previous
import numpy as np
from contextlib import ExitStack

import concourse.bass as bass
import concourse.mybir as mybir
from concourse.bass_utils import run_bass_kernel_spmd

F32 = mybir.dt.float32
BF16 = mybir.dt.bfloat16
AF = mybir.ActivationFunctionType
ALU = mybir.AluOpType
AX = mybir.AxisListType

D = 1024
DC = 8
NCORES = 8
S_FULL = 2048
MODE = "fused"

ENGS = ("t", "v", "s", "g", "y")


class Prog:

    def __init__(self, sems):
        self.sems = sems
        self.cnt = {k: 0 for k in sems}
        self.ops = {e: [] for e in ENGS}
        self.last = {}
        self.hw = {e: {k: 0 for k in sems} for e in ENGS}

    def _wait(self, eng, sem, val):
        if self.hw[eng][sem] >= val:
            return
        self.hw[eng][sem] = val
        h = self.sems[sem]
        self.ops[eng].append(lambda E, h=h, v=val: E.wait_ge(h, v))

    DMA_SEMS = ("ld", "ldw", "ldw2", "hw")

    def need(self, eng, keys):
        for k in keys:
            if k in self.last:
                sem, c = self.last[k]
                if sem in self.DMA_SEMS:
                    c = self.cnt[sem]
                self._wait(eng, sem, c)

    def emit(self, eng, fns, needs=(), inc=None, produces=(), amt=1):
        if not isinstance(fns, (list, tuple)):
            fns = [fns]
        self.need(eng, needs)
        if inc in self.DMA_SEMS:
            self._wait(eng, inc, self.cnt[inc])
        for fn in fns[:-1]:
            self.ops[eng].append(fn)
        if inc is None:
            assert not produces
            self.ops[eng].append(fns[-1])
            return
        self.cnt[inc] += amt
        c = self.cnt[inc]
        h = self.sems[inc]
        self.ops[eng].append(lambda E, fn=fns[-1], h=h, a=amt: fn(E).then_inc(h, a))
        for k in produces:
            self.last[k] = (inc, c)

    def alias(self, dst_key, src_key):
        if src_key in self.last:
            self.last[dst_key] = self.last[src_key]


class KB:
    def __init__(self, S):
        assert S % 512 == 0
        self.S = S
        self.NT4 = S // 512
        self.NT = S // 128
        self.HS = S // 2
        self.nc = bass.Bass("TRN2", target_bir_lowering=False)
        self.es = ExitStack()
        self.inp = {}

    def dram_in(self, name, shape):
        h = self.nc.dram_tensor(name, list(shape), F32, kind="ExternalInput")
        self.inp[name] = h
        return h.ap()

    def dram_out(self, name, shape):
        h = self.nc.dram_tensor(name, list(shape), F32, kind="ExternalOutput")
        return h.ap()

    def alloc(self):
        nc, es, S = self.nc, self.es, self.S
        sb = lambda n, sh, dt: es.enter_context(nc.sbuf_tensor(n, sh, dt))
        ps = lambda n, sh, dt: es.enter_context(nc.psum_tensor(n, sh, dt))
        DCS = DC * S
        self.xs = sb("xs", [128, DCS], BF16)
        self.B = sb("Bx", [128, max(DCS, 16 * 512 + 3 * S + 1024)], BF16)
        self.A = sb("Ax", [128, 2 * DCS + DC * 2048], BF16)
        self.VT = sb("VTx", [128, max(DCS, 16 * 1024)], BF16)
        self.sm = sb("smx", [128, 4096], BF16)
        self.gtb = sb("gtb", [128, 8], BF16)
        self.gsc = sb("gsc", [128, 64], F32)
        self.b1T = sb("b1T", [128, 320], F32)
        self.b2s = sb("b2s", [8, 1024], BF16)
        self.sel = sb("sel_sb", [8, 1024], BF16)
        self.baT = sb("baT", [128, 32], F32)
        self.brow = sb("brow", [1, 8], BF16)
        self.lrow = sb("lrow", [1, 4], BF16)
        self.bgrow = sb("bgrow", [1, 8], BF16)
        self.lwsb = sb("lwsb", [128, 32], BF16)
        self.rwsb = sb("rwsb", [128, 64], BF16)
        self.wgsb = sb("wgsb", [128, 64], BF16)
        self.ones = sb("ones", [128, 128], BF16)
        self.idb = sb("idb", [128, 128], BF16)
        self.idf = sb("idf", [128, 128], F32)
        self.mask = sb("mask_sb", [128, 128], BF16)
        self.PS_A = ps("PS_A", [128, 2048], F32)
        self.PS_B = ps("PS_B", [128, 1024], F32)
        self.PS_T = ps("PS_T", [128, 2048], BF16)
        names = ["pe", "dve", "act", "pool", "ld", "ldw", "ldw2", "hw"]
        sems = {n: es.enter_context(nc.semaphore(n)) for n in names}
        self.P = Prog(sems)
        self.B_f32 = self.B[:, :].bitcast(F32)
        self.A_f32 = self.A[:, :].bitcast(F32)
        self.xs_f32 = self.xs[:, :].bitcast(F32)

    def xs_t(self, dc, c0, w):
        return self.xs[:, dc * self.S + c0: dc * self.S + c0 + w]

    def sb_t(self, dc, c0, w):
        return self.B[:, dc * self.S + c0: dc * self.S + c0 + w]

    def A_t(self, blk, c0, w):
        return self.A[:, blk * self.S + c0: blk * self.S + c0 + w]

    def gfm_t(self, c0, w):
        g0 = 16 * 512 + 2 * self.S + 1024
        return self.B[0:8, g0 + c0: g0 + c0 + w]

    def pT(self, slot):
        return self.PS_T[:, slot * 128:(slot + 1) * 128]


    def p_consts(self, need_mask, need_sel, need_idf):
        P = self.P
        ident = self.dram_in("c_ident", [128, 128])
        P.emit("g", lambda E, s=self, a=ident: E.dma_start(s.idb[:, :], a),
               inc="ld", produces=["idb"], amt=16)
        if need_idf:
            P.emit("g", lambda E, s=self, a=ident: E.dma_start(s.idf[:, :], a),
                   inc="ld", produces=["idf"], amt=16)
        if need_mask:
            m = self.dram_in("c_mask", [128, 128])
            P.emit("g", lambda E, s=self, a=m: E.dma_start(s.mask[:, :], a),
                   inc="ld", produces=["mask"], amt=16)
        if need_sel:
            sl = self.dram_in("c_sel", [8, 1024])
            P.emit("g", lambda E, s=self, a=sl: E.dma_start(s.sel[:, :], a),
                   inc="ld", produces=["sel"], amt=16)
        P.emit("g", lambda E, s=self: E.memset(s.ones[:, :], 1.0),
               inc="pool", produces=["ones"])

    def p_load_x(self):
        P, S = self.P, self.S
        x = self.dram_in("x_in", [S, D])
        for tc in range(self.NT):
            sk = "stg%d" % (tc % 2)
            stg = self.sm[:, (tc % 2) * 1024:(tc % 2) * 1024 + 1024]
            P.emit("g",
                   lambda E, d=stg, a=x[tc * 128:(tc + 1) * 128, :]: E.dma_start(d, a),
                   needs=["f:" + sk], inc="ld", produces=[sk], amt=16)
            bank = tc % 2
            tps, cps = [], []
            for dc in range(8):
                slot = bank * 8 + dc
                tps.append(lambda E, o=self.pT(slot),
                           i=stg[:, dc * 128:(dc + 1) * 128], s=self:
                           E.transpose(o, i, s.idb[:, :]))
                cps.append(lambda E, o=self.xs_t(dc, tc * 128, 128),
                           i=self.pT(slot): E.tensor_copy(o, i))
            P.emit("t", tps, needs=[sk, "idb", "fT%d" % bank,
                                    "fT%dv" % bank, "fT%ds" % bank],
                   inc="pe", produces=["trx"])
            P.emit("v", cps, needs=["trx"], inc="dve",
                   produces=["xs", "fT%d" % bank, "fT%dv" % bank,
                             "fT%ds" % bank])
            P.alias("f:" + sk, "trx")

    def _load_wbuf(self, w_ap, tag, ncols, slot=0):
        P = self.P
        src = w_ap.rearrange("(dc p) m -> p dc m", p=128)
        base = slot * 8192
        dst = self.VT[:, base:base + DC * ncols].rearrange(
            "p (dc m) -> p dc m", dc=DC)
        P.emit("g", lambda E, d=dst, a=src: E.dma_start(d, a),
               needs=["f:wbuf%d" % slot], inc="ldw", produces=[tag], amt=16)

    def p_proj(self, w_ap, tag, src_t, src_key, dst_t=None, dst_key=None,
               bias_col=None, bias_key=None, resid=False, ncols=1024, slot=0,
               out_scale=None):
        P = self.P
        self._load_wbuf(w_ap, tag, ncols, slot=slot)
        base = slot * 8192
        noc = ncols // 128
        for oc in range(noc):
            for nt in range(self.NT4):
                pk = "pB%d" % (oc % 2)
                pslice = self.PS_B[:, (oc % 2) * 512:(oc % 2) * 512 + 512]
                mms = []
                for dc in range(DC):
                    mms.append(lambda E, o=pslice,
                               l=self.VT[:, base + dc * ncols + oc * 128:
                                         base + dc * ncols + oc * 128 + 128],
                               r=src_t(dc, nt * 512, 512),
                               st=(dc == 0), sp=(dc == DC - 1):
                               E.matmul(o, l, r, start=st, stop=sp))
                mk = "mm:%s" % tag
                P.emit("t", mms, needs=[tag, src_key, "f:" + pk],
                       inc="pe", produces=[mk])
                if resid:
                    if bias_col is not None:
                        P.emit("v", lambda E, o=self.sb_t(oc, nt * 512, 512),
                               i=pslice, b=bias_col(oc),
                               x=self.xs_t(oc, nt * 512, 512):
                               E.scalar_tensor_tensor(o, i, b, x, ALU.add, ALU.add),
                               needs=[mk, "xs", bias_key], inc="dve",
                               produces=["sb", "f:" + pk])
                    else:
                        P.emit("v", lambda E, o=self.sb_t(oc, nt * 512, 512),
                               i=pslice, x=self.xs_t(oc, nt * 512, 512):
                               E.tensor_tensor(o, i, x, ALU.add),
                               needs=[mk, "xs"], inc="dve",
                               produces=["sb", "f:" + pk])
                else:
                    if bias_col is not None:
                        P.emit("v", lambda E, o=dst_t(oc, nt * 512, 512),
                               i=pslice, b=bias_col(oc), sc=out_scale:
                               (E.tensor_scalar(o, i, b, None, ALU.add)
                                if sc is None else
                                E.tensor_scalar(o, i, b, sc, ALU.add, ALU.mult)),
                               needs=[mk, bias_key, "sb"], inc="dve",
                               produces=[dst_key, "f:" + pk])
                    else:
                        P.emit("s", lambda E, o=dst_t(oc, nt * 512, 512),
                               i=pslice, sc=out_scale:
                               (E.copy(o, i) if sc is None else
                                E.activation(o, i, AF.Copy, scale=sc)),
                               needs=[mk, "sb"], inc="act",
                               produces=[dst_key, "f:" + pk])
        P.alias("f:wbuf%d" % slot, "mm:%s" % tag)

    def p_build_vt(self, vkey):
        P = self.P
        for kc in range(self.NT):
            bank = kc % 2
            tps, cps = [], []
            for hc in range(8):
                slot = bank * 8 + hc
                tps.append(lambda E, o=self.pT(slot),
                           i=self.A_t(16 + hc, kc * 128, 128), s=self:
                           E.transpose(o, i, s.idb[:, :]))
                cps.append(lambda E,
                           o=self.VT[:, kc * 1024 + hc * 128:
                                     kc * 1024 + hc * 128 + 128],
                           i=self.pT(slot): E.tensor_copy(o, i))
            P.emit("t", tps, needs=[vkey, "idb", "f:wbuf0", "f:wbuf1",
                                    "fT%d" % bank, "fT%dv" % bank,
                                    "fT%ds" % bank],
                   inc="pe", produces=["trv"])
            P.emit("v", cps, needs=["trv"], inc="dve",
                   produces=["VT", "fT%d" % bank, "fT%dv" % bank,
                             "fT%ds" % bank])

    def _softmax(self, kn, scale, e_sb, attn_bf, needs_key, free_keys):
        P = self.P
        g = self.gsc
        P.emit("v", lambda E, s=self, kn=kn: E.reduce_max(
            g[:, 60:61], s.PS_A[:, 0:kn], axis=AX.X, negate=True),
            needs=[needs_key], inc="dve", produces=["nmax"])
        P.emit("v", lambda E, sc=scale, g=g: E.tensor_scalar(
            g[:, 61:62], g[:, 60:61], sc, None, ALU.mult),
            needs=["nmax"], inc="dve", produces=["biasS"])
        P.emit("s", lambda E, s=self, o=e_sb[:, 0:kn], kn=kn, sc=scale, g=g:
               E.activation(o, s.PS_A[:, 0:kn], AF.Exp, bias=g[:, 61:62], scale=sc),
               needs=["biasS"], inc="act", produces=["e"] + free_keys)
        P.emit("v", lambda E, i=e_sb[:, 0:kn], g=g: E.reduce_sum(
            g[:, 62:63], i, axis=AX.X),
            needs=["e"], inc="dve", produces=["ssum"])
        P.emit("v", lambda E, g=g: E.reciprocal(g[:, 63:64], g[:, 62:63]),
               needs=["ssum"], inc="dve", produces=["rsum"])
        P.emit("v", lambda E, o=attn_bf[:, 0:kn], i=e_sb[:, 0:kn], g=g:
               E.tensor_scalar(o, i, g[:, 63:64], None, ALU.mult),
               needs=["rsum"], inc="dve", produces=["attnq"])
        return "attnq"

    def p_attn_l0(self):
        P, S = self.P, self.S

        def raw(par):
            return self.B[:, par * S:par * S + S]

        def scl(par):
            return self.B[:, (2 + 2 * par) * S:(2 + 2 * par) * S + S]

        aT = self.B[:, 3 * S:4 * S]

        def emit_scores(i):
            qt = i
            par = i % 2
            g = self.gsc
            s0 = 16 + 16 * par
            for ns in range(self.NT4):
                pslice = self.PS_A[:, ns * 512:ns * 512 + 512]
                mms = []
                for dc in range(DC):
                    mms.append(lambda E, o=pslice,
                               l=self.A_t(dc, qt * 128, 128),
                               r=self.A_t(8 + dc, ns * 512, 512),
                               st=(dc == 0), sp=(dc == DC - 1):
                               E.matmul(o, l, r, start=st, stop=sp))
                P.emit("t", mms, needs=["q", "k", "f:pA%d" % ns],
                       inc="pe", produces=["sc%d.%d" % (par, ns)])
                if ns % 2 == 1 or ns == self.NT4 - 1:
                    m = ns // 2
                    w2 = min(1024, S - m * 1024)
                    mneeds = ["sc%d.%d" % (par, ns)]
                    if ns % 2 == 1:
                        mneeds.append("sc%d.%d" % (par, ns - 1))
                    P.emit("v", lambda E, g=g, o=g[:, s0 + m:s0 + m + 1],
                           i_=self.PS_A[:, m * 1024:m * 1024 + w2]:
                           E.reduce_max(o, i_, axis=AX.X),
                           needs=mneeds, inc="dve",
                           produces=["smx%d.%d" % (par, m)])

        def emit_softmax(i):
            par = i % 2
            g = self.gsc
            s0 = 16 + 16 * par
            nm = (self.NT4 + 1) // 2
            P.emit("v", lambda E, g=g, s0=s0, nm=nm: E.reduce_max(
                g[:, s0 + 8:s0 + 9], g[:, s0:s0 + nm], axis=AX.X, negate=True),
                needs=["smx%d.%d" % (par, m) for m in range(nm)],
                inc="dve", produces=["nmax%d" % par])
            for m in range(nm):
                w2 = min(1024, S - m * 1024)
                fr = ["f:pA%d" % (2 * m)]
                if w2 > 512:
                    fr.append("f:pA%d" % (2 * m + 1))
                P.emit("s", lambda E, s=self, m=m, w2=w2,
                       o=raw(par)[:, m * 1024:m * 1024 + w2], g=g, s0=s0:
                       E.activation(o, s.PS_A[:, m * 1024:m * 1024 + w2],
                                    AF.Exp, bias=g[:, s0 + 8:s0 + 9], scale=1.0,
                                    accum_out=g[:, s0 + 9 + m:s0 + 10 + m]),
                       needs=["nmax%d" % par] +
                             (["f:raw%d" % par] if m == 0 else []),
                       inc="act",
                       produces=["e%d.%d" % (par, m)] + fr)
            if nm > 1:
                P.emit("v", lambda E, g=g, s0=s0, nm=nm: E.reduce_sum(
                    g[:, s0 + 14:s0 + 15], g[:, s0 + 9:s0 + 9 + nm], axis=AX.X),
                    needs=["e%d.%d" % (par, m) for m in range(nm)],
                    inc="dve", produces=["ssum%d" % par])
                P.emit("v", lambda E, g=g, s0=s0: E.reciprocal(
                    g[:, s0 + 15:s0 + 16], g[:, s0 + 14:s0 + 15]),
                    needs=["ssum%d" % par], inc="dve", produces=["rsum%d" % par])
            else:
                P.emit("v", lambda E, g=g, s0=s0: E.reciprocal(
                    g[:, s0 + 15:s0 + 16], g[:, s0 + 9:s0 + 10]),
                    needs=["e%d.0" % par], inc="dve", produces=["rsum%d" % par])
            P.emit("v", lambda E, o=scl(par)[:, 0:S], i_=raw(par)[:, 0:S],
                   g=g, s0=s0:
                   E.tensor_scalar(o, i_, g[:, s0 + 15:s0 + 16], None, ALU.mult),
                   needs=["rsum%d" % par, "f:scl%d" % par], inc="dve",
                   produces=["attnq%d" % par, "f:raw%d" % par])

        def emit_tail(i):
            qt = i
            par = i % 2
            for g0 in range(0, self.NT, 8):
                gn = min(8, self.NT - g0)
                bank = (g0 // 8) % 2
                tps = []
                for i_ in range(gn):
                    kc = g0 + i_
                    slot = bank * 8 + i_
                    tps.append(lambda E, o=self.pT(slot),
                               i=scl(par)[:, kc * 128:(kc + 1) * 128], s=self:
                               E.transpose(o, i, s.idb[:, :]))
                P.emit("t", tps, needs=["attnq%d" % par, "idb",
                                        "fT%dv" % bank, "fT%ds" % bank],
                       inc="pe", produces=["trp"])
                use_v = (i + g0 // 8) % 2 == 0
                cps = []
                for i_ in range(gn):
                    kc = g0 + i_
                    slot = bank * 8 + i_
                    if use_v:
                        cps.append(lambda E, o=aT[:, kc * 128:(kc + 1) * 128],
                                   i=self.pT(slot): E.tensor_copy(o, i))
                    else:
                        cps.append(lambda E, o=aT[:, kc * 128:(kc + 1) * 128],
                                   i=self.pT(slot): E.copy(o, i))
                P.emit("v" if use_v else "s", cps, needs=["trp"],
                       inc="dve" if use_v else "act",
                       produces=["fT%dv" % bank, "fT%ds" % bank] +
                                ["aTc%d" % (g0 + i_) for i_ in range(gn)])
            for bank in range(2):
                mms = []
                for hc in range(4 * bank, 4 * bank + 4):
                    pdst = self.PS_B[:, hc * 128:(hc + 1) * 128]
                    for kc in range(self.NT):
                        mms.append(lambda E, o=pdst,
                                   l=self.VT[:, kc * 1024 + hc * 128:
                                             kc * 1024 + hc * 128 + 128],
                                   r=aT[:, kc * 128:(kc + 1) * 128],
                                   st=(kc == 0), sp=(kc == self.NT - 1):
                                   E.matmul(o, l, r, start=st, stop=sp))
                P.emit("t", mms,
                       needs=["VT", "f:pB%d" % bank] +
                             ["aTc%d" % kc for kc in range(self.NT)],
                       inc="pe", produces=["ctxp%d" % bank])
                octx = self.A[:, 0:24 * S].rearrange("p (blk s) -> p blk s",
                                                     blk=24)[
                    :, 16 + 4 * bank:16 + 4 * bank + 4,
                    qt * 128:(qt + 1) * 128]
                ictx = self.PS_B[:, 4 * bank * 128:
                                 (4 * bank + 4) * 128].rearrange(
                    "p (a b) -> p a b", a=4)
                P.emit("s", lambda E, o=octx, i=ictx: E.copy(o, i),
                       needs=["ctxp%d" % bank], inc="act",
                       produces=["ctx", "f:pB%d" % bank] +
                                (["f:scl%d" % par] if bank == 1 else []))

        n = self.NT
        for i in range(n + 1):
            if i < n:
                emit_scores(i)
            if i > 0:
                emit_tail(i - 1)
            if i < n:
                emit_softmax(i)
        P.alias("f:wbuf0", "ctxp1")
        P.alias("f:wbuf1", "ctxp1")
        P.alias("f:w1q0", "ctxp1")
        P.alias("f:w1q1", "ctxp1")
        P.alias("f:w2b", "ctxp1")

    def p_attn_l1(self):
        P, S = self.P, self.S
        units = [(j, qt) for j in range(16) for qt in range(self.NT)]

        def raw(par):
            return self.B[:, par * S:par * S + S]

        def scl(par):
            return self.B[:, (2 + 2 * par) * S:(2 + 2 * par) * S + S]

        aT = self.B[:, 3 * S:4 * S]

        def emit_scores(i):
            j, qt = units[i]
            dc = j // 2
            boff = 64 * (j % 2)
            kn = (qt + 1) * 128
            nsl = (kn + 511) // 512
            par = i % 2
            g = self.gsc
            s0 = 16 + 16 * par
            base = 0
            b0 = 0
            for ns in range(nsl):
                w = min(512, kn - ns * 512)
                pslice = self.PS_A[:, base + ns * 512:base + ns * 512 + w]
                mms = [lambda E, o=pslice,
                       l=self.A[boff:boff + 64,
                                dc * S + qt * 128: dc * S + qt * 128 + 128],
                       r=self.A[boff:boff + 64,
                                (8 + dc) * S + ns * 512:
                                (8 + dc) * S + ns * 512 + w],
                       sp=(ns != nsl - 1):
                       E.matmul(o, l, r, start=True, stop=sp)]
                if ns == nsl - 1:
                    mms.append(lambda E, s=self,
                               o=self.PS_A[:, base + qt * 128:
                                           base + qt * 128 + 128]:
                               E.matmul(o, s.idb[:, :], s.mask[:, :],
                                        start=False, stop=True))
                P.emit("t", mms,
                       needs=["q", "k", "f:pA%d" % (b0 + ns)] +
                             (["idb", "mask"] if ns == nsl - 1 else []),
                       inc="pe", produces=["sc%d.%d" % (par, ns)])
                if ns % 2 == 1 or ns == nsl - 1:
                    m = ns // 2
                    w2 = min(1024, kn - m * 1024)
                    mneeds = ["sc%d.%d" % (par, ns)]
                    if ns % 2 == 1:
                        mneeds.append("sc%d.%d" % (par, ns - 1))
                    P.emit("v", lambda E, g=g, o=g[:, s0 + m:s0 + m + 1],
                           i_=self.PS_A[:, base + m * 1024:
                                        base + m * 1024 + w2]:
                           E.reduce_max(o, i_, axis=AX.X),
                           needs=mneeds, inc="dve",
                           produces=["smx%d.%d" % (par, m)])

        def emit_softmax(i):
            j, qt = units[i]
            kn = (qt + 1) * 128
            nsl = (kn + 511) // 512
            par = i % 2
            g = self.gsc
            s0 = 16 + 16 * par
            nm = (nsl + 1) // 2
            P.emit("v", lambda E, g=g, s0=s0, nm=nm: E.reduce_max(
                g[:, s0 + 8:s0 + 9], g[:, s0:s0 + nm], axis=AX.X, negate=True),
                needs=["smx%d.%d" % (par, m) for m in range(nm)],
                inc="dve", produces=["nmax%d" % par])
            base = 0
            b0 = 0
            for m in range(nm):
                w2 = min(1024, kn - m * 1024)
                fr = ["f:pA%d" % (b0 + 2 * m)]
                if w2 > 512:
                    fr.append("f:pA%d" % (b0 + 2 * m + 1))
                P.emit("s", lambda E, s=self, m=m, w2=w2, base=base,
                       o=raw(par)[:, m * 1024:m * 1024 + w2], g=g, s0=s0:
                       E.activation(o, s.PS_A[:, base + m * 1024:
                                              base + m * 1024 + w2],
                                    AF.Exp, bias=g[:, s0 + 8:s0 + 9], scale=1.0,
                                    accum_out=g[:, s0 + 9 + m:s0 + 10 + m]),
                       needs=["nmax%d" % par] +
                             (["f:raw%d" % par] if m == 0 else []),
                       inc="act",
                       produces=["e%d.%d" % (par, m)] + fr)
            if nm > 1:
                P.emit("v", lambda E, g=g, s0=s0, nm=nm: E.reduce_sum(
                    g[:, s0 + 14:s0 + 15], g[:, s0 + 9:s0 + 9 + nm], axis=AX.X),
                    needs=["e%d.%d" % (par, m) for m in range(nm)],
                    inc="dve", produces=["ssum%d" % par])
                P.emit("v", lambda E, g=g, s0=s0: E.reciprocal(
                    g[:, s0 + 15:s0 + 16], g[:, s0 + 14:s0 + 15]),
                    needs=["ssum%d" % par], inc="dve", produces=["rsum%d" % par])
            else:
                P.emit("v", lambda E, g=g, s0=s0: E.reciprocal(
                    g[:, s0 + 15:s0 + 16], g[:, s0 + 9:s0 + 10]),
                    needs=["e%d.0" % par], inc="dve", produces=["rsum%d" % par])
            P.emit("v", lambda E, o=scl(par)[:, 0:kn], i_=raw(par)[:, 0:kn],
                   g=g, s0=s0:
                   E.tensor_scalar(o, i_, g[:, s0 + 15:s0 + 16], None, ALU.mult),
                   needs=["rsum%d" % par, "f:scl%d" % par], inc="dve",
                   produces=["attnq%d" % par, "f:raw%d" % par])

        ctx_cnt = 0

        def emit_tail(i):
            nonlocal ctx_cnt
            j, qt = units[i]
            dc = j // 2
            boff = 64 * (j % 2)
            par = i % 2
            for g0 in range(0, qt + 1, 8):
                gn = min(8, qt + 1 - g0)
                bank = (g0 // 8) % 2
                tps = []
                for i_ in range(gn):
                    kc = g0 + i_
                    slot = bank * 8 + i_
                    tps.append(lambda E, o=self.pT(slot),
                               i=scl(par)[:, kc * 128:(kc + 1) * 128], s=self:
                               E.transpose(o, i, s.idb[:, :]))
                P.emit("t", tps, needs=["attnq%d" % par, "idb",
                                        "fT%dv" % bank, "fT%ds" % bank],
                       inc="pe", produces=["trp"])
                use_v = (i + g0 // 8) % 2 == 0
                cps = []
                for i_ in range(gn):
                    kc = g0 + i_
                    slot = bank * 8 + i_
                    if use_v:
                        cps.append(lambda E, o=aT[:, kc * 128:(kc + 1) * 128],
                                   i=self.pT(slot): E.tensor_copy(o, i))
                    else:
                        cps.append(lambda E, o=aT[:, kc * 128:(kc + 1) * 128],
                                   i=self.pT(slot): E.copy(o, i))
                P.emit("v" if use_v else "s", cps, needs=["trp"],
                       inc="dve" if use_v else "act",
                       produces=["fT%dv" % bank, "fT%ds" % bank] +
                                ["aTc%d" % (g0 + i_) for i_ in range(gn)])
            cslot = ctx_cnt % 2
            ctx_cnt += 1
            pdst = self.PS_B[0:64, cslot * 512:cslot * 512 + 128]
            mms = []
            for kc in range(qt + 1):
                mms.append(lambda E, o=pdst,
                           l=self.VT[:, kc * 1024 + 64 * j:
                                     kc * 1024 + 64 * j + 64],
                           r=aT[:, kc * 128:(kc + 1) * 128],
                           st=(kc == 0), sp=(kc == qt):
                           E.matmul(o, l, r, start=st, stop=sp))
            P.emit("t", mms,
                   needs=["VT", "f:pB%d" % cslot] +
                         ["aTc%d" % kc for kc in range(qt + 1)],
                   inc="pe", produces=["ctxp"])
            P.emit("s", lambda E,
                   o=self.A[boff:boff + 64,
                            (16 + dc) * S + qt * 128:
                            (16 + dc) * S + qt * 128 + 128],
                   i=pdst: E.copy(o, i),
                   needs=["ctxp"], inc="act",
                   produces=["ctx", "f:pB%d" % cslot, "f:scl%d" % par])

        n = len(units)
        for i in range(n + 1):
            if i < n:
                emit_scores(i)
            if i > 0:
                emit_tail(i - 1)
            if i < n:
                emit_softmax(i)
        P.alias("f:pB0", "ctx")
        P.alias("f:pB1", "ctx")
        P.alias("f:wbuf0", "ctxp")
        P.alias("f:wbuf1", "ctxp")
        P.alias("f:w1q0", "ctxp")
        P.alias("f:w1q1", "ctxp")
        P.alias("f:w2b", "ctxp")

    def p_ln(self, out_ap=None):
        P, S, HS = self.P, self.S, self.HS
        mrow = self.sm[0:1, 0:HS]
        rrow = self.sm[0:1, HS:2 * HS]
        m2row = self.sm[0:1, 2 * HS:3 * HS]
        sq = self.sm[:, 3 * 1024:3 * 1024 + min(HS, 1024)]
        inv_d = 1.0 / 1024.0
        nsl = (HS + 511) // 512
        for th in range(2):
            t0 = th * HS
            lastk = "xs" if out_ap is None else "stgc"
            if th > 0:
                for b in range(4):
                    P.alias("f:pA%d" % b, lastk)
                for b in range(2):
                    P.alias("f:pB%d" % b, lastk)
            for dc in range(DC):
                P.emit("v", lambda E, o=sq[:, 0:HS],
                       a=self.sb_t(dc, t0, HS):
                       E.tensor_tensor(o, a, a, ALU.mult),
                       needs=["sb", "f:sq", "f:tmp"], inc="dve", produces=["sqd"])
                for ns in range(nsl):
                    w = min(512, HS - ns * 512)
                    P.emit("t", lambda E,
                           o=self.PS_A[0:1, ns * 512:ns * 512 + w],
                           l=self.ones[:, 0:1],
                           r=self.sb_t(dc, t0 + ns * 512, w),
                           st=(dc == 0), sp=(dc == DC - 1):
                           E.matmul(o, l, r, start=st, stop=sp),
                           needs=(["sb", "ones", "f:pA0", "f:pA1"]
                                  if (dc == 0 and ns == 0) else ["sb"]),
                           inc="pe" if dc == DC - 1 else None,
                           produces=["psum"] if dc == DC - 1 else [])
                for ns in range(nsl):
                    w = min(512, HS - ns * 512)
                    P.emit("t", lambda E,
                           o=self.PS_A[0:1, 1024 + ns * 512: 1024 + ns * 512 + w],
                           l=self.ones[:, 0:1], r=sq[:, ns * 512:ns * 512 + w],
                           st=(dc == 0), sp=(dc == DC - 1):
                           E.matmul(o, l, r, start=st, stop=sp),
                           needs=(["sqd", "f:pA2", "f:pA3"]
                                  if (dc == 0 and ns == 0) else ["sqd"]),
                           inc="pe", produces=["sqrd"] + (["psq"] if dc == DC - 1 else []))
                P.alias("f:sq", "sqrd")
            P.emit("v", lambda E, o=mrow, s=self, iv=inv_d: E.tensor_scalar(
                o, s.PS_A[0:1, 0:HS], iv, None, ALU.mult),
                needs=["psum"], inc="dve", produces=["mrow", "f:pA0", "f:pA1"])
            P.emit("v", lambda E, s=self, iv=inv_d: E.tensor_scalar(
                s.PS_A[0:1, 1024:1024 + HS], s.PS_A[0:1, 1024:1024 + HS], iv, None, ALU.mult),
                needs=["psq"], inc="dve", produces=["msq"])
            P.emit("v", lambda E, o=m2row, i=mrow: E.tensor_tensor(o, i, i, ALU.mult),
                   needs=["mrow"], inc="dve", produces=["m2"])
            P.emit("v", lambda E, s=self, i=m2row: E.tensor_tensor(
                s.PS_A[0:1, 1024:1024 + HS], s.PS_A[0:1, 1024:1024 + HS], i, ALU.subtract),
                needs=["msq", "m2"], inc="dve", produces=["var"])
            P.emit("v", lambda E, s=self: E.tensor_scalar(
                s.PS_A[0:1, 1024:1024 + HS], s.PS_A[0:1, 1024:1024 + HS], 1e-5, None, ALU.add),
                needs=["var"], inc="dve", produces=["vare"])
            P.emit("s", lambda E, s=self: E.activation(
                s.PS_A[0:1, 1024:1024 + HS], s.PS_A[0:1, 1024:1024 + HS], AF.Ln),
                needs=["vare"], inc="act", produces=["lnv"])
            P.emit("s", lambda E, s=self, o=rrow: E.activation(
                o, s.PS_A[0:1, 1024:1024 + HS], AF.Exp, scale=-0.5),
                needs=["lnv"], inc="act", produces=["rrow", "f:pA2", "f:pA3"])
            for ns in range(nsl):
                w = min(512, HS - ns * 512)
                P.emit("t", lambda E, o=self.PS_B[:, ns * 512:ns * 512 + w],
                       l=self.ones[0:1, 0:128],
                       r=mrow[0:1, ns * 512:ns * 512 + w]:
                       E.matmul(o, l, r, start=True, stop=True),
                       needs=["mrow", "ones", "f:pB%d" % ns],
                       inc="pe", produces=["mbc%d" % ns])
                P.emit("t", lambda E, o=self.PS_A[:, ns * 512:ns * 512 + w],
                       l=self.ones[0:1, 0:128],
                       r=rrow[0:1, ns * 512:ns * 512 + w]:
                       E.matmul(o, l, r, start=True, stop=True),
                       needs=["rrow", "ones", "f:pA%d" % ns],
                       inc="pe", produces=["rbc%d" % ns])
            if out_ap is None:
                tmp = self.sm[:, 3 * 1024:3 * 1024 + 512]
                for dc in range(DC):
                    for nb in range(nsl):
                        w = min(512, HS - nb * 512)
                        P.emit("v", lambda E, o=tmp[:, 0:w],
                               a=self.sb_t(dc, t0 + nb * 512, w),
                               m=self.PS_B[:, nb * 512:nb * 512 + w]:
                               E.tensor_tensor(o, a, m, ALU.subtract),
                               needs=["mbc%d" % nb, "sb", "f:tmp", "sqrd"],
                               inc="dve", produces=["tmpd"])
                        P.emit("v", lambda E,
                               o=self.xs_t(dc, t0 + nb * 512, w),
                               a=tmp[:, 0:w],
                               r=self.PS_A[:, nb * 512:nb * 512 + w]:
                               E.tensor_tensor(o, a, r, ALU.mult),
                               needs=["rbc%d" % nb, "tmpd"],
                               inc="dve", produces=["xs", "f:tmp"])
            else:
                for tcl in range(HS // 128):
                    tc = th * (HS // 128) + tcl
                    roomy = (DC * self.S // 2) >= 3072
                    sp = (tc % 2) if roomy else 0
                    stage = self.xs_f32[:, sp * 1024:sp * 1024 + 1024]
                    fb = 2048 if roomy else 1024
                    fxy8 = self.xs_f32[:, fb:fb + 1024]
                    nb = (tcl * 128) // 512
                    ci = tcl * 128 - nb * 512
                    subs, mults = [], []
                    for dc in range(DC):
                        fx = fxy8[:, dc * 128:(dc + 1) * 128]
                        subs.append(lambda E, o=fx,
                                    a=self.sb_t(dc, t0 + tcl * 128, 128),
                                    m=self.PS_B[:, nb * 512 + ci:
                                                nb * 512 + ci + 128]:
                                    E.tensor_tensor(o, a, m, ALU.subtract))
                        mults.append(lambda E, o=fx, a=fx,
                                     r=self.PS_A[:, nb * 512 + ci:
                                                 nb * 512 + ci + 128]:
                                     E.tensor_tensor(o, a, r, ALU.mult))
                    P.emit("v", subs, needs=["mbc%d" % nb, "sb", "f:fx"],
                           inc="dve", produces=["fxs"])
                    P.emit("v", mults, needs=["rbc%d" % nb, "fxs"],
                           inc="dve", produces=["fxm"])
                    tps = []
                    for dc in range(DC):
                        fx = fxy8[:, dc * 128:(dc + 1) * 128]
                        pdst = self.PS_A[:, 1024 + dc * 128:1024 + dc * 128 + 128]
                        tps.append(lambda E, o=pdst, i=fx, s=self:
                                   E.transpose(o, i, s.idf[:, :]))
                    P.emit("t", tps, needs=["fxm", "idf", "f:ot"],
                           inc="pe", produces=["otr"])
                    cps = []
                    for dc in range(DC):
                        pdst = self.PS_A[:, 1024 + dc * 128:1024 + dc * 128 + 128]
                        cps.append(lambda E, o=stage[:, dc * 128:(dc + 1) * 128],
                                   i=pdst: E.tensor_copy(o, i))
                    P.emit("v", cps, needs=["otr", "f:stgo%d" % sp], inc="dve",
                           produces=["f:ot", "stgc", "f:fx"])
                    if sp == 0:
                        P.emit("y", lambda E, o=out_ap[tc * 128:(tc + 1) * 128, :],
                               i=stage: E.dma_start(o, i),
                               needs=["stgc"], inc="hw",
                               produces=["f:stgo%d" % sp], amt=16)
                    else:
                        P.emit("g", lambda E, o=out_ap[tc * 128:(tc + 1) * 128, :],
                               i=stage: E.dma_start(o, i),
                               needs=["stgc"], inc="ld",
                               produces=["f:stgo%d" % sp], amt=16)
        P.alias("f:sm_rows", "rbc%d" % (nsl - 1))
        lastk = "xs" if out_ap is None else "stgc"
        for b in range(4):
            P.alias("f:pA%d" % b, lastk)
        for b in range(2):
            P.alias("f:pB%d" % b, lastk)
        if out_ap is None:
            P.alias("f:sm_tmp", "xs")
        else:
            P.alias("f:sm_tmp", "rbc%d" % (nsl - 1))
            P._wait("y", "hw", P.cnt["hw"])
            P._wait("g", "ld", P.cnt["ld"])

    def p_gate(self, l1, pfx):
        P, S = self.P, self.S
        g = self.gsc
        if not l1:
            rw = self.dram_in("l0_rwT", [D, 8])
            rb = self.dram_in("l0_rb", [1, 8])
            lw = self.dram_in("l0_lwT", [D, 4])
            lb = self.dram_in("l0_lb", [1, 4])
            P.emit("g", lambda E, s=self,
                   a=rw.rearrange("(dc p) e -> p dc e", p=128):
                   E.dma_start(s.rwsb[:, :].rearrange("p (dc e) -> p dc e", dc=8), a), inc="ld", produces=["rw"], amt=16)
            P.emit("g", lambda E, s=self,
                   a=lw.rearrange("(dc p) e -> p dc e", p=128):
                   E.dma_start(s.lwsb[:, :].rearrange("p (dc e) -> p dc e", dc=8), a), inc="ld", produces=["lwk"], amt=16)
            P.emit("g", lambda E, s=self, a=rb: E.dma_start(s.brow[:, :], a),
                   inc="ld", produces=["brow"], amt=16)
            P.emit("g", lambda E, s=self, a=lb: E.dma_start(s.lrow[:, :], a),
                   inc="ld", produces=["lrow"], amt=16)
            wsb, wk, brow_sb, bk, wc = self.rwsb, "rw", self.brow, "brow", 8
        else:
            wgap = self.dram_in("l1_wg", [D, 8])
            bg = self.dram_in("l1_bg", [1, 8])
            P.emit("g", lambda E, s=self,
                   a=wgap.rearrange("(dc p) e -> p dc e", p=128):
                   E.dma_start(s.wgsb[:, :].rearrange("p (dc e) -> p dc e", dc=8), a), inc="ld", produces=["wgk"], amt=16)
            P.emit("g", lambda E, s=self, a=bg: E.dma_start(s.bgrow[:, :], a),
                   inc="ld", produces=["bgrow"], amt=16)
            wsb, wk, brow_sb, bk, wc = self.wgsb, "wgk", self.bgrow, "bgrow", 8

        def ch(eng, fn, needs=(), produces=()):
            sem = {"v": "dve", "s": "act"}[eng]
            P.emit(eng, fn, needs=list(needs) + ["ch"], inc=sem,
                   produces=list(produces) + ["ch"])

        for tt in range(self.NT):
            mms = []
            for dc in range(DC):
                mms.append(lambda E, o=self.PS_A[:, 0:8],
                           l=self.xs_t(dc, tt * 128, 128),
                           r=wsb[:, dc * wc:dc * wc + 8],
                           st=(dc == 0): E.matmul(o, l, r, start=st, stop=False))
            mms.append(lambda E, o=self.PS_A[:, 0:8], s=self, br=brow_sb:
                       E.matmul(o, s.ones[0:1, 0:128], br[0:1, 0:8],
                                start=False, stop=True))
            P.emit("t", mms, needs=["xs", wk, bk, "ones", "f:pA0"],
                   inc="pe", produces=["rlp"])
            if not l1:
                mms = []
                for dc in range(DC):
                    mms.append(lambda E, o=self.PS_A[:, 8:12],
                               l=self.xs_t(dc, tt * 128, 128),
                               r=self.lwsb[:, dc * 4:dc * 4 + 4],
                               st=(dc == 0): E.matmul(o, l, r, start=st, stop=False))
                mms.append(lambda E, o=self.PS_A[:, 8:12], s=self:
                           E.matmul(o, s.ones[0:1, 0:128], s.lrow[0:1, 0:4],
                                    start=False, stop=True))
                P.emit("t", mms, needs=["xs", "lwk", "lrow", "ones"],
                       inc="pe", produces=["lgp"])
            if "ch" not in P.last:
                P.last["ch"] = P.last["rlp"]
            if not l1:
                ch("v", lambda E, s=self, g=g: E.tensor_copy(g[:, 0:4], s.PS_A[:, 8:12]),
                   needs=["lgp"])
                ch("v", lambda E, g=g: E.reduce_max(g[:, 60:61], g[:, 0:4],
                                                    axis=AX.X, negate=True))
                ch("s", lambda E, g=g: E.activation(g[:, 4:8], g[:, 0:4], AF.Exp,
                                                    bias=g[:, 60:61], scale=1.0))
                ch("v", lambda E, g=g: E.reduce_sum(g[:, 61:62], g[:, 4:8], axis=AX.X))
                ch("v", lambda E, g=g: E.reciprocal(g[:, 62:63], g[:, 61:62]))
                ch("v", lambda E, g=g: E.tensor_scalar(g[:, 8:12], g[:, 4:8],
                                                       g[:, 62:63], 0.1,
                                                       ALU.mult, ALU.mult))
                ch("v", lambda E, g=g: E.tensor_copy(g[:, 12:16], g[:, 8:12]))
                ch("v", lambda E, g=g: E.tensor_copy(g[:, 16:20], g[:, 8:12]))
                ch("v", lambda E, g=g: E.tensor_tensor(g[:, 12:15], g[:, 12:15],
                                                       g[:, 9:12], ALU.add))
                ch("v", lambda E, g=g: E.tensor_tensor(g[:, 16:19], g[:, 16:19],
                                                       g[:, 9:12], ALU.add))
                ch("v", lambda E, s=self, g=g: E.tensor_tensor(
                    g[:, 20:28], s.PS_A[:, 0:8], g[:, 12:20], ALU.add),
                   needs=["rlp"], produces=["f:pA0"])
            else:
                ch("v", lambda E, s=self, g=g: E.tensor_copy(g[:, 20:28], s.PS_A[:, 0:8]),
                   needs=["rlp"], produces=["f:pA0"])
            ch("v", lambda E, g=g: E.reduce_max(g[:, 60:61], g[:, 20:28],
                                                axis=AX.X, negate=True))
            ch("s", lambda E, g=g: E.activation(g[:, 28:36], g[:, 20:28], AF.Exp,
                                                bias=g[:, 60:61], scale=1.0))
            ch("v", lambda E, g=g: E.reduce_sum(g[:, 61:62], g[:, 28:36], axis=AX.X))
            ch("v", lambda E, g=g: E.reciprocal(g[:, 62:63], g[:, 61:62]))
            ch("v", lambda E, g=g: E.tensor_scalar(g[:, 36:44], g[:, 28:36],
                                                   g[:, 62:63], None, ALU.mult))
            if not l1:
                ch("v", lambda E, g=g: E.tensor_scalar(g[:, 44:52], g[:, 20:28],
                                                       g[:, 60:61], 0.0,
                                                       ALU.add, ALU.is_ge))
                ch("v", lambda E, g=g: E.tensor_scalar(g[:, 44:52], g[:, 44:52],
                                                       -30000.0, None, ALU.mult))
                ch("v", lambda E, g=g: E.tensor_tensor(g[:, 44:52], g[:, 44:52],
                                                       g[:, 20:28], ALU.add))
                ch("v", lambda E, g=g: E.reduce_max(g[:, 59:60], g[:, 44:52],
                                                    axis=AX.X, negate=True))
                ch("v", lambda E, g=g: E.tensor_scalar(g[:, 44:52], g[:, 20:28],
                                                       g[:, 59:60], 0.0,
                                                       ALU.add, ALU.is_ge))
                ch("v", lambda E, g=g: E.tensor_tensor(g[:, 52:60], g[:, 36:44],
                                                       g[:, 44:52], ALU.mult))
                ch("v", lambda E, g=g: E.reduce_sum(g[:, 61:62], g[:, 52:60], axis=AX.X))
                ch("v", lambda E, g=g: E.reciprocal(g[:, 62:63], g[:, 61:62]))
                ch("v", lambda E, g=g: E.tensor_scalar(g[:, 52:60], g[:, 52:60],
                                                       g[:, 62:63], None, ALU.mult))
            else:
                ch("v", lambda E, g=g: E.tensor_scalar(g[:, 44:52], g[:, 20:28],
                                                       g[:, 60:61], 0.0,
                                                       ALU.add, ALU.is_ge))
                ch("v", lambda E, g=g: E.tensor_tensor(g[:, 52:60], g[:, 36:44],
                                                       g[:, 44:52], ALU.mult))
            ch("v", lambda E, s=self, g=g: E.tensor_copy(s.gtb[:, :], g[:, 52:60]),
               produces=["gtbd"])
            P.emit("t", lambda E, s=self: E.transpose(
                s.PS_T[0:8, 0:128], s.gtb[:, :], s.idb[:, :]),
                needs=["gtbd", "idb", "fT0v", "fT0s"], inc="pe",
                produces=["gtr"])
            P.emit("v", lambda E, s=self, c0=tt * 128: E.tensor_copy(
                s.gfm_t(c0, 128), s.PS_T[0:8, 0:128]),
                needs=["gtr", "psum", "psq", "xs"], inc="dve",
                produces=["gfm", "fT0", "fT0v", "fT0s", "ch"])

    def p_ffn(self, l1, pfx):
        P, S = self.P, self.S
        if not l1:
            w1_ap = self.dram_in("l0_w1", [8, D, 4096])
            b1_ap = self.dram_in("l0_b1", [8, 4096])
            w2_ap = self.dram_in("l0_w2", [8, 4096, D])
            b2_ap = self.dram_in("l0_b2", [8, D])
            HF, nhalf = 4096, 2
            b1off = 0
        else:
            w1_ap = self.dram_in("l1_w1", [8, D, D])
            b1_ap = self.dram_in("l1_b1", [8, D])
            w2_ap = self.dram_in("l1_w2", [8, D, D])
            b2_ap = self.dram_in("l1_b2", [8, D])
            HF, nhalf = 1024, 1
            b1off = 256
        hfc = HF // 128 // nhalf
        hw = HF // nhalf
        P.emit("g", lambda E, s=self, a=b2_ap: E.dma_start(s.b2s[:, :], a),
               needs=["f:b2s"], inc="ld", produces=["b2s"], amt=16)
        b1sb = self.sm[0:8, 0:HF]
        P.emit("g", lambda E, o=b1sb, a=b1_ap: E.dma_start(o, a),
               needs=["f:b1sb", "f:stg0", "f:stg1", "f:sm_rows", "f:sm_tmp"],
               inc="ld", produces=["b1sb"], amt=16)
        for c in range(HF // 128):
            P.emit("t", lambda E, s=self, i=b1sb[0:8, c * 128:(c + 1) * 128]:
                   E.transpose(s.PS_T[0:128, 0:8], i, s.idb[0:8, 0:8]),
                   needs=["b1sb", "idb", "fT0v", "fT0s"], inc="pe",
                   produces=["b1tr"])
            P.emit("v", lambda E, s=self, c0=b1off + c * 8: E.tensor_copy(
                s.b1T[:, c0:c0 + 8], s.PS_T[0:128, 0:8]),
                needs=["b1tr"], inc="dve",
                produces=["b1T", "fT0", "fT0v", "fT0s"])
        ACC = self.A_f32
        for oc in range(DC):
            for nt in range(self.NT4):
                pk = "pB%d" % (nt % 2)
                pslice = self.PS_B[:, (nt % 2) * 512:(nt % 2) * 512 + 512]
                P.emit("t", lambda E, o=pslice,
                       l=self.b2s[0:8, oc * 128:(oc + 1) * 128],
                       r=self.gfm_t(nt * 512, 512):
                       E.matmul(o, l, r, start=True, stop=True),
                       needs=["b2s", "gfm", "f:" + pk], inc="pe",
                       produces=["b2m"])
                P.emit("v", lambda E,
                       o=ACC[:, oc * S + nt * 512: oc * S + nt * 512 + 512],
                       i=pslice: E.tensor_copy(o, i),
                       needs=["b2m"], inc="dve",
                       produces=["AC%d.%d" % (oc, nt), "f:" + pk])
        g_bc = self.B_f32[:, 4096:4096 + S]
        tmpb = 16 * 512 + 2 * S
        for e in range(8):
            for nt in range(self.NT4):
                P.emit("t", lambda E, s=self, o=self.PS_A[:, 0:512],
                       l=self.sel[0:8, e * 128:(e + 1) * 128],
                       r=self.gfm_t(nt * 512, 512):
                       E.matmul(o, l, r, start=True, stop=True),
                       needs=["sel", "gfm", "f:pA0"], inc="pe", produces=["gbp"])
                P.emit("v", lambda E, o=g_bc[:, nt * 512:nt * 512 + 512],
                       i=self.PS_A[:, 0:512]: E.tensor_copy(o, i),
                       needs=["gbp", "gbcrd", "xs"], inc="dve",
                       produces=["gbc", "f:pA0"])
            for half in range(nhalf):
                hq = hw // 2
                for q in range(2):
                    w1src = w1_ap[e, :, half * hw + q * hq:
                                  half * hw + q * hq + hq].rearrange(
                        "(dc p) h -> p dc h", p=128)
                    w1dst = self.A[:, 2 * DC * S: 2 * DC * S + DC * hw].rearrange(
                        "p (dc h) -> p dc h", dc=DC)[:, :, q * hq:(q + 1) * hq]
                    P.emit("g", lambda E, o=w1dst, a=w1src: E.dma_start(o, a),
                           needs=["f:w1q%d" % q], inc="ldw",
                           produces=["w1q%d" % q], amt=16)
                w2src = w2_ap[e, half * hw:half * hw + hw, :].rearrange(
                    "(hc p) d -> p hc d", p=128)
                w2dst = self.VT[:, 0:hfc * 1024].rearrange(
                    "p (hc d) -> p hc d", hc=hfc)
                P.emit("g", lambda E, o=w2dst, a=w2src: E.dma_start(o, a),
                       needs=["f:w2b", "f:wbuf0", "f:wbuf1"],
                       inc="ldw2", produces=["w2b"], amt=16)
                for tt in range(self.NT4):
                    for hc in range(hfc):
                        pk = "pA%d" % (hc % 2)
                        pslice = self.PS_A[:, (hc % 2) * 512:(hc % 2) * 512 + 512]
                        mms = []
                        for dc in range(DC):
                            mms.append(lambda E, o=pslice,
                                       l=self.A[:, 2 * DC * S + dc * hw + hc * 128:
                                                2 * DC * S + dc * hw + hc * 128 + 128],
                                       r=self.xs_t(dc, tt * 512, 512),
                                       st=(dc == 0), sp=(dc == DC - 1):
                                       E.matmul(o, l, r, start=st, stop=sp))
                        qq = 0 if hc < hfc // 2 else 1
                        prod = ["hp%d" % (hc % 2)]
                        if tt == self.NT4 - 1 and hc == hfc // 2 - 1:
                            prod.append("w1q0rd")
                        if tt == self.NT4 - 1 and hc == hfc - 1:
                            prod.append("w1q1rd")
                        P.emit("t", mms, needs=["w1q%d" % qq, "xs", "f:" + pk],
                               inc="pe", produces=prod)
                        hcg = half * hfc + hc
                        tmp = self.B[:, tmpb + (hc % 2) * 512:
                                     tmpb + (hc % 2) * 512 + 512]
                        P.emit("s", lambda E, o=tmp, i=pslice, s=self,
                               bc=b1off + hcg * 8 + e:
                               E.activation(o, i, AF.Gelu,
                                            bias=s.b1T[:, bc:bc + 1], scale=1.0),
                               needs=["hp%d" % (hc % 2), "b1T",
                                      "f:tm%d" % (hc % 2)],
                               inc="act", produces=["ge%d" % (hc % 2), "f:" + pk])
                        P.emit("v", lambda E, o=self.B[:, hc * 512:(hc + 1) * 512],
                               i=tmp, gb=g_bc[:, tt * 512:tt * 512 + 512]:
                               E.tensor_tensor(o, i, gb, ALU.mult),
                               needs=["ge%d" % (hc % 2), "gbc", "f:hsb"],
                               inc="dve",
                               produces=["h%d" % hc, "f:tm%d" % (hc % 2)])
                    for oc in range(DC):
                        pk = "pB%d" % (oc % 2)
                        pslice = self.PS_B[:, (oc % 2) * 512:(oc % 2) * 512 + 512]
                        mms = []
                        for hc in range(hfc):
                            mms.append(lambda E, o=pslice,
                                       l=self.VT[:, hc * 1024 + oc * 128:
                                                 hc * 1024 + oc * 128 + 128],
                                       r=self.B[:, hc * 512:(hc + 1) * 512],
                                       st=(hc == 0), sp=(hc == hfc - 1):
                                       E.matmul(o, l, r, start=st, stop=sp))
                        P.emit("t", mms,
                               needs=["w2b", "f:" + pk] +
                                     ["h%d" % hc for hc in range(hfc)],
                               inc="pe", produces=["op"])
                        P.emit("v", lambda E,
                               o=ACC[:, oc * S + tt * 512: oc * S + tt * 512 + 512],
                               i=pslice: E.tensor_tensor(o, i, o, ALU.add),
                               needs=["op", "AC%d.%d" % (oc, tt)], inc="dve",
                               produces=["AC%d.%d" % (oc, tt), "f:" + pk])
                    P.alias("f:hsb", "op")
                P.alias("f:w1q0", "w1q0rd")
                P.alias("f:w1q1", "w1q1rd")
                P.alias("f:w2b", "op")
            P.alias("gbcrd", "op")
        for dc in range(DC):
            for nt in range(self.NT4):
                P.emit("v", lambda E, o=self.sb_t(dc, nt * 512, 512),
                       a=ACC[:, dc * S + nt * 512: dc * S + nt * 512 + 512],
                       x=self.xs_t(dc, nt * 512, 512):
                       E.tensor_tensor(o, a, x, ALU.add),
                       needs=["AC%d.%d" % (dc, nt), "xs"], inc="dve",
                       produces=["sb"])
        P.alias("f:b2s", "b2m")
        P.alias("f:b1sb", "b1tr")
        P.alias("f:wbuf0", "op")
        P.alias("f:wbuf1", "op")


    def emit_l0a(self):
        P = self.P
        WqT = self.dram_in("l0_WqT", [D, D])
        WkT = self.dram_in("l0_WkT", [D, D])
        WvT = self.dram_in("l0_WvT", [D, D])
        WoT = self.dram_in("l0_WoT", [D, D])
        xsrc = lambda dc, c0, w: self.xs_t(dc, c0, w)
        for sl, (tag, wap, b0, dk) in enumerate((("0q", WqT, 0, "q"),
                                                 ("0k", WkT, 8, "k"),
                                                 ("0v", WvT, 16, "v"))):
            self.p_proj(wap, tag, xsrc, "xs",
                        dst_t=(lambda oc, c0, w, b=b0: self.A_t(b + oc, c0, w)),
                        dst_key=dk, slot=sl % 2,
                        out_scale=(1.0 / 32.0 if dk == "q" else None))
        self.p_build_vt("v")
        self.p_attn_l0()
        self.p_proj(WoT, "0o", (lambda dc, c0, w: self.A_t(16 + dc, c0, w)),
                    "ctx", resid=True, slot=1)

    def emit_l1a(self):
        P = self.P
        Wa = self.dram_in("l1_Wa", [D, 3 * D])
        ba = self.dram_in("l1_ba", [24, 128])
        Wp = self.dram_in("l1_Wp", [D, D])
        bp = self.dram_in("l1_bp", [8, 128])
        stg = self.sm[0:24, 0:128]
        P.emit("g", lambda E, o=stg, a=ba: E.dma_start(o, a),
               needs=["f:b1sb", "f:stg0", "f:stg1", "f:sm_rows", "f:sm_tmp"],
               inc="ld", produces=["bast"], amt=16)
        P.emit("t", lambda E, s=self, i=stg: E.transpose(
            s.PS_T[0:128, 0:24], i, s.idb[0:24, 0:24]),
            needs=["bast", "idb", "fT0v", "fT0s"], inc="pe",
            produces=["batr"])
        P.emit("v", lambda E, s=self: E.tensor_copy(
            s.baT[:, 0:24], s.PS_T[0:128, 0:24]),
            needs=["batr"], inc="dve",
            produces=["baT", "fT0", "fT0v", "fT0s"])
        stg2 = self.sm[0:8, 128:256]
        P.emit("g", lambda E, o=stg2, a=bp: E.dma_start(o, a),
               needs=["f:b1sb", "f:stg0", "f:stg1", "f:sm_rows", "f:sm_tmp"],
               inc="ld", produces=["bpst"], amt=16)
        P.emit("t", lambda E, s=self, i=stg2: E.transpose(
            s.PS_T[0:128, 0:8], i, s.idb[0:8, 0:8]),
            needs=["bpst", "idb", "fT0v", "fT0s"], inc="pe",
            produces=["bptr"])
        P.emit("v", lambda E, s=self: E.tensor_copy(
            s.baT[:, 24:32], s.PS_T[0:128, 0:8]),
            needs=["bptr"], inc="dve",
            produces=["bpT", "fT0", "fT0v", "fT0s"])
        P.alias("f:b1sb", "bptr")
        xsrc = lambda dc, c0, w: self.xs_t(dc, c0, w)
        for sec, (tag, b0, dk) in enumerate((("1q", 0, "q"), ("1k", 8, "k"),
                                             ("1v", 16, "v"))):
            wsec = Wa[:, sec * D:(sec + 1) * D]
            self.p_proj(wsec, tag, xsrc, "xs",
                        dst_t=(lambda oc, c0, w, b=b0: self.A_t(b + oc, c0, w)),
                        dst_key=dk,
                        bias_col=(lambda oc, s0=sec * 8: self.baT[:, s0 + oc:s0 + oc + 1]),
                        bias_key="baT", slot=sec % 2,
                        out_scale=(0.125 if dk == "q" else None))
        self.p_build_vt("v")
        self.p_attn_l1()
        self.p_proj(Wp, "1o", (lambda dc, c0, w: self.A_t(16 + dc, c0, w)),
                    "ctx", resid=True,
                    bias_col=(lambda oc: self.baT[:, 24 + oc:24 + oc + 1]),
                    bias_key="bpT", slot=1)

    def finalize(self):
        P = self.P
        nc = self.nc
        with nc.Block() as block:
            @block.tensor
            def _(E):
                for f in P.ops["t"]:
                    f(E)

            @block.vector
            def _(E):
                for f in P.ops["v"]:
                    f(E)

            @block.scalar
            def _(E):
                for f in P.ops["s"]:
                    f(E)

            @block.gpsimd
            def _(E):
                for f in P.ops["g"]:
                    f(E)

            @block.sync
            def _(E):
                for f in P.ops["y"]:
                    f(E)
        self.es.close()
        return nc


def build_program(parts, S):
    kb = KB(S)
    kb.alloc()
    need_mask = "l1a" in parts
    need_sel = ("l0f" in parts) or ("l1f" in parts)
    kb.p_consts(need_mask, need_sel, need_idf=True)
    kb.p_load_x()
    out_ap = kb.dram_out("x_out", [S, D])
    for i, part in enumerate(parts):
        last = (i == len(parts) - 1)
        if part == "l0a":
            kb.emit_l0a()
        elif part == "l0f":
            kb.p_gate(False, "0")
            kb.p_ffn(False, "0")
        elif part == "l1a":
            kb.emit_l1a()
        elif part == "l1f":
            kb.p_gate(True, "1")
            kb.p_ffn(True, "1")
        kb.p_ln(out_ap=out_ap if last else None)
    return kb



def _prep(inputs):
    f = lambda a: np.ascontiguousarray(a, dtype=np.float32)
    w = {}
    W_qkv = inputs["L0_attn_W_qkv"]
    w["l0_WqT"] = f(W_qkv[0::3, :].T)
    w["l0_WkT"] = f(W_qkv[1::3, :].T)
    w["l0_WvT"] = f(W_qkv[2::3, :].T)
    w["l0_WoT"] = f(inputs["L0_attn_W_out"].T)
    w["l0_rwT"] = f(inputs["L0_ffn_router_w"].T)
    w["l0_rb"] = f(inputs["L0_ffn_router_b"].reshape(1, 8))
    w["l0_lwT"] = f(inputs["L0_ffn_lang_w"].T)
    w["l0_lb"] = f(inputs["L0_ffn_lang_b"].reshape(1, 4))
    w["l0_w1"] = f(inputs["L0_ffn_w1"])
    w["l0_b1"] = f(inputs["L0_ffn_b1"])
    w["l0_w2"] = f(inputs["L0_ffn_w2"])
    w["l0_b2"] = f(inputs["L0_ffn_b2"])
    w["l1_Wa"] = f(inputs["L1_attn_W_attn"])
    w["l1_ba"] = f(inputs["L1_attn_b_attn"].reshape(24, 128))
    w["l1_Wp"] = f(inputs["L1_attn_W_proj"])
    w["l1_bp"] = f(inputs["L1_attn_b_proj"].reshape(8, 128))
    w["l1_wg"] = f(inputs["L1_ffn_Wg"])
    w["l1_bg"] = f(inputs["L1_ffn_bg"].reshape(1, 8))
    w["l1_w1"] = f(inputs["L1_ffn_W1"])
    w["l1_b1"] = f(inputs["L1_ffn_b1"])
    w["l1_w2"] = f(inputs["L1_ffn_W2"])
    w["l1_b2"] = f(inputs["L1_ffn_b2"])
    ii, jj = np.meshgrid(np.arange(128), np.arange(128), indexing="ij")
    w["c_mask"] = np.where(jj <= ii, 0.0, -30000.0).astype(np.float32)
    w["c_ident"] = np.eye(128, dtype=np.float32)
    sel = np.zeros((8, 1024), dtype=np.float32)
    for e in range(8):
        sel[e, e * 128:(e + 1) * 128] = 1.0
    w["c_sel"] = sel
    return w


_STAGE_PARTS = {
    "staged4": [["l0a"], ["l0f"], ["l1a"], ["l1f"]],
    "fused": [["l0a", "l0f", "l1a", "l1f"]],
}


def kernel(**inputs):
    w = _prep(inputs)
    x = np.ascontiguousarray(inputs["L0_attn_x"], dtype=np.float32)
    B = x.shape[0]
    xcur = [np.ascontiguousarray(x[c]) for c in range(B)]
    for parts in _STAGE_PARTS[MODE]:
        kb = build_program(parts, S_FULL)
        nc = kb.finalize()
        names = set(kb.inp.keys())
        in_maps = []
        for c in range(B):
            m = {k: w[k] for k in names if k in w}
            if "x_in" in names:
                m["x_in"] = xcur[c]
            in_maps.append(m)
        res = run_bass_kernel_spmd(nc, in_maps, core_ids=list(range(NCORES)))
        xcur = [np.ascontiguousarray(res.results[c]["x_out"]) for c in range(B)]
    return np.stack(xcur, axis=0)


# version 21
# speedup vs baseline: 1.2201x; 1.0346x over previous
import numpy as np
from contextlib import ExitStack

import concourse.bass as bass
import concourse.mybir as mybir
from concourse.bass_utils import run_bass_kernel_spmd

F32 = mybir.dt.float32
BF16 = mybir.dt.bfloat16
AF = mybir.ActivationFunctionType
ALU = mybir.AluOpType
AX = mybir.AxisListType

D = 1024
DC = 8
NCORES = 8
S_FULL = 2048
MODE = "fused"

ENGS = ("t", "v", "s", "g", "y")


class Prog:

    def __init__(self, sems):
        self.sems = sems
        self.cnt = {k: 0 for k in sems}
        self.ops = {e: [] for e in ENGS}
        self.last = {}
        self.hw = {e: {k: 0 for k in sems} for e in ENGS}

    def _wait(self, eng, sem, val):
        if self.hw[eng][sem] >= val:
            return
        self.hw[eng][sem] = val
        h = self.sems[sem]
        self.ops[eng].append(lambda E, h=h, v=val: E.wait_ge(h, v))

    DMA_SEMS = ("ld", "ldw", "ldw2", "hw")

    def need(self, eng, keys):
        for k in keys:
            if k in self.last:
                sem, c = self.last[k]
                if sem in self.DMA_SEMS:
                    c = self.cnt[sem]
                self._wait(eng, sem, c)

    def emit(self, eng, fns, needs=(), inc=None, produces=(), amt=1):
        if not isinstance(fns, (list, tuple)):
            fns = [fns]
        self.need(eng, needs)
        if inc in self.DMA_SEMS:
            self._wait(eng, inc, self.cnt[inc])
        for fn in fns[:-1]:
            self.ops[eng].append(fn)
        if inc is None:
            assert not produces
            self.ops[eng].append(fns[-1])
            return
        self.cnt[inc] += amt
        c = self.cnt[inc]
        h = self.sems[inc]
        self.ops[eng].append(lambda E, fn=fns[-1], h=h, a=amt: fn(E).then_inc(h, a))
        for k in produces:
            self.last[k] = (inc, c)

    def alias(self, dst_key, src_key):
        if src_key in self.last:
            self.last[dst_key] = self.last[src_key]


class KB:
    def __init__(self, S):
        assert S % 512 == 0
        self.S = S
        self.NT4 = S // 512
        self.NT = S // 128
        self.HS = S // 2
        self.nc = bass.Bass("TRN2", target_bir_lowering=False)
        self.es = ExitStack()
        self.inp = {}

    def dram_in(self, name, shape):
        h = self.nc.dram_tensor(name, list(shape), F32, kind="ExternalInput")
        self.inp[name] = h
        return h.ap()

    def dram_out(self, name, shape):
        h = self.nc.dram_tensor(name, list(shape), F32, kind="ExternalOutput")
        return h.ap()

    def alloc(self):
        nc, es, S = self.nc, self.es, self.S
        sb = lambda n, sh, dt: es.enter_context(nc.sbuf_tensor(n, sh, dt))
        ps = lambda n, sh, dt: es.enter_context(nc.psum_tensor(n, sh, dt))
        DCS = DC * S
        self.xs = sb("xs", [128, DCS], BF16)
        self.B = sb("Bx", [128, max(DCS, 16 * 512 + 3 * S + 1024)], BF16)
        self.A = sb("Ax", [128, 2 * DCS + DC * 2048], BF16)
        self.VT = sb("VTx", [128, max(DCS, 16 * 1024)], BF16)
        self.sm = sb("smx", [128, 4096], BF16)
        self.gtb = sb("gtb", [128, 128], BF16)
        self.gsc = sb("gsc", [128, 64], F32)
        self.b1T = sb("b1T", [128, 320], F32)
        self.b2s = sb("b2s", [8, 1024], BF16)
        self.sel = sb("sel_sb", [8, 1024], BF16)
        self.baT = sb("baT", [128, 32], F32)
        self.brow = sb("brow", [1, 8], BF16)
        self.lrow = sb("lrow", [1, 4], BF16)
        self.bgrow = sb("bgrow", [1, 8], BF16)
        self.lwsb = sb("lwsb", [128, 32], BF16)
        self.rwsb = sb("rwsb", [128, 64], BF16)
        self.wgsb = sb("wgsb", [128, 64], BF16)
        self.ones = sb("ones", [128, 128], BF16)
        self.idb = sb("idb", [128, 128], BF16)
        self.idf = sb("idf", [128, 128], F32)
        self.mask = sb("mask_sb", [128, 128], BF16)
        self.PS_A = ps("PS_A", [128, 2048], F32)
        self.PS_B = ps("PS_B", [128, 1024], F32)
        self.PS_T = ps("PS_T", [128, 2048], BF16)
        names = ["pe", "dve", "act", "pool", "ld", "ldw", "ldw2", "hw"]
        sems = {n: es.enter_context(nc.semaphore(n)) for n in names}
        self.P = Prog(sems)
        self.B_f32 = self.B[:, :].bitcast(F32)
        self.A_f32 = self.A[:, :].bitcast(F32)
        self.xs_f32 = self.xs[:, :].bitcast(F32)

    def xs_t(self, dc, c0, w):
        return self.xs[:, dc * self.S + c0: dc * self.S + c0 + w]

    def sb_t(self, dc, c0, w):
        return self.B[:, dc * self.S + c0: dc * self.S + c0 + w]

    def A_t(self, blk, c0, w):
        return self.A[:, blk * self.S + c0: blk * self.S + c0 + w]

    def gfm_t(self, c0, w):
        g0 = 16 * 512 + 2 * self.S + 1024
        return self.B[0:8, g0 + c0: g0 + c0 + w]

    def pT(self, slot):
        return self.PS_T[:, slot * 128:(slot + 1) * 128]


    def p_consts(self, need_mask, need_sel, need_idf):
        P = self.P
        ident = self.dram_in("c_ident", [128, 128])
        P.emit("g", lambda E, s=self, a=ident: E.dma_start(s.idb[:, :], a),
               inc="ld", produces=["idb"], amt=16)
        if need_idf:
            P.emit("g", lambda E, s=self, a=ident: E.dma_start(s.idf[:, :], a),
                   inc="ld", produces=["idf"], amt=16)
        if need_mask:
            m = self.dram_in("c_mask", [128, 128])
            P.emit("g", lambda E, s=self, a=m: E.dma_start(s.mask[:, :], a),
                   inc="ld", produces=["mask"], amt=16)
        if need_sel:
            sl = self.dram_in("c_sel", [8, 1024])
            P.emit("g", lambda E, s=self, a=sl: E.dma_start(s.sel[:, :], a),
                   inc="ld", produces=["sel"], amt=16)
        P.emit("g", lambda E, s=self: E.memset(s.ones[:, :], 1.0),
               inc="pool", produces=["ones"])

    def p_load_x(self):
        P, S = self.P, self.S
        x = self.dram_in("x_in", [S, D])
        for tc in range(self.NT):
            sk = "stg%d" % (tc % 2)
            stg = self.sm[:, (tc % 2) * 1024:(tc % 2) * 1024 + 1024]
            P.emit("g",
                   lambda E, d=stg, a=x[tc * 128:(tc + 1) * 128, :]: E.dma_start(d, a),
                   needs=["f:" + sk], inc="ld", produces=[sk], amt=16)
            bank = tc % 2
            tps, cps = [], []
            for dc in range(8):
                slot = bank * 8 + dc
                tps.append(lambda E, o=self.pT(slot),
                           i=stg[:, dc * 128:(dc + 1) * 128], s=self:
                           E.transpose(o, i, s.idb[:, :]))
                cps.append(lambda E, o=self.xs_t(dc, tc * 128, 128),
                           i=self.pT(slot): E.tensor_copy(o, i))
            P.emit("t", tps, needs=[sk, "idb", "fT%d" % bank,
                                    "fT%dv" % bank, "fT%ds" % bank],
                   inc="pe", produces=["trx"])
            P.emit("v", cps, needs=["trx"], inc="dve",
                   produces=["xs", "fT%d" % bank, "fT%dv" % bank,
                             "fT%ds" % bank])
            P.alias("f:" + sk, "trx")

    def _load_wbuf(self, w_ap, tag, ncols, slot=0):
        P = self.P
        src = w_ap.rearrange("(dc p) m -> p dc m", p=128)
        base = slot * 8192
        dst = self.VT[:, base:base + DC * ncols].rearrange(
            "p (dc m) -> p dc m", dc=DC)
        P.emit("g", lambda E, d=dst, a=src: E.dma_start(d, a),
               needs=["f:wbuf%d" % slot], inc="ldw", produces=[tag], amt=16)

    def p_proj(self, w_ap, tag, src_t, src_key, dst_t=None, dst_key=None,
               bias_col=None, bias_key=None, resid=False, ncols=1024, slot=0,
               out_scale=None):
        P = self.P
        self._load_wbuf(w_ap, tag, ncols, slot=slot)
        base = slot * 8192
        noc = ncols // 128
        for oc in range(noc):
            for nt in range(self.NT4):
                pk = "pB%d" % (oc % 2)
                pslice = self.PS_B[:, (oc % 2) * 512:(oc % 2) * 512 + 512]
                mms = []
                for dc in range(DC):
                    mms.append(lambda E, o=pslice,
                               l=self.VT[:, base + dc * ncols + oc * 128:
                                         base + dc * ncols + oc * 128 + 128],
                               r=src_t(dc, nt * 512, 512),
                               st=(dc == 0), sp=(dc == DC - 1):
                               E.matmul(o, l, r, start=st, stop=sp))
                mk = "mm:%s" % tag
                P.emit("t", mms, needs=[tag, src_key, "f:" + pk],
                       inc="pe", produces=[mk])
                if resid:
                    if bias_col is not None:
                        P.emit("v", lambda E, o=self.sb_t(oc, nt * 512, 512),
                               i=pslice, b=bias_col(oc),
                               x=self.xs_t(oc, nt * 512, 512):
                               E.scalar_tensor_tensor(o, i, b, x, ALU.add, ALU.add),
                               needs=[mk, "xs", bias_key], inc="dve",
                               produces=["sb", "f:" + pk])
                    else:
                        P.emit("v", lambda E, o=self.sb_t(oc, nt * 512, 512),
                               i=pslice, x=self.xs_t(oc, nt * 512, 512):
                               E.tensor_tensor(o, i, x, ALU.add),
                               needs=[mk, "xs"], inc="dve",
                               produces=["sb", "f:" + pk])
                else:
                    if bias_col is not None:
                        P.emit("v", lambda E, o=dst_t(oc, nt * 512, 512),
                               i=pslice, b=bias_col(oc), sc=out_scale:
                               (E.tensor_scalar(o, i, b, None, ALU.add)
                                if sc is None else
                                E.tensor_scalar(o, i, b, sc, ALU.add, ALU.mult)),
                               needs=[mk, bias_key, "sb"], inc="dve",
                               produces=[dst_key, "f:" + pk])
                    else:
                        P.emit("s", lambda E, o=dst_t(oc, nt * 512, 512),
                               i=pslice, sc=out_scale:
                               (E.copy(o, i) if sc is None else
                                E.activation(o, i, AF.Copy, scale=sc)),
                               needs=[mk, "sb"], inc="act",
                               produces=[dst_key, "f:" + pk])
        P.alias("f:wbuf%d" % slot, "mm:%s" % tag)

    def p_build_vt(self, vkey):
        P = self.P
        for kc in range(self.NT):
            bank = kc % 2
            tps, cps = [], []
            for hc in range(8):
                slot = bank * 8 + hc
                tps.append(lambda E, o=self.pT(slot),
                           i=self.A_t(16 + hc, kc * 128, 128), s=self:
                           E.transpose(o, i, s.idb[:, :]))
                cps.append(lambda E,
                           o=self.VT[:, kc * 1024 + hc * 128:
                                     kc * 1024 + hc * 128 + 128],
                           i=self.pT(slot): E.tensor_copy(o, i))
            P.emit("t", tps, needs=[vkey, "idb", "f:wbuf0", "f:wbuf1",
                                    "fT%d" % bank, "fT%dv" % bank,
                                    "fT%ds" % bank],
                   inc="pe", produces=["trv"])
            P.emit("v", cps, needs=["trv"], inc="dve",
                   produces=["VT", "fT%d" % bank, "fT%dv" % bank,
                             "fT%ds" % bank])

    def _softmax(self, kn, scale, e_sb, attn_bf, needs_key, free_keys):
        P = self.P
        g = self.gsc
        P.emit("v", lambda E, s=self, kn=kn: E.reduce_max(
            g[:, 60:61], s.PS_A[:, 0:kn], axis=AX.X, negate=True),
            needs=[needs_key], inc="dve", produces=["nmax"])
        P.emit("v", lambda E, sc=scale, g=g: E.tensor_scalar(
            g[:, 61:62], g[:, 60:61], sc, None, ALU.mult),
            needs=["nmax"], inc="dve", produces=["biasS"])
        P.emit("s", lambda E, s=self, o=e_sb[:, 0:kn], kn=kn, sc=scale, g=g:
               E.activation(o, s.PS_A[:, 0:kn], AF.Exp, bias=g[:, 61:62], scale=sc),
               needs=["biasS"], inc="act", produces=["e"] + free_keys)
        P.emit("v", lambda E, i=e_sb[:, 0:kn], g=g: E.reduce_sum(
            g[:, 62:63], i, axis=AX.X),
            needs=["e"], inc="dve", produces=["ssum"])
        P.emit("v", lambda E, g=g: E.reciprocal(g[:, 63:64], g[:, 62:63]),
               needs=["ssum"], inc="dve", produces=["rsum"])
        P.emit("v", lambda E, o=attn_bf[:, 0:kn], i=e_sb[:, 0:kn], g=g:
               E.tensor_scalar(o, i, g[:, 63:64], None, ALU.mult),
               needs=["rsum"], inc="dve", produces=["attnq"])
        return "attnq"

    def p_attn_l0(self):
        P, S = self.P, self.S

        def raw(par):
            return self.B[:, par * S:par * S + S]

        def scl(par):
            return self.B[:, (2 + 2 * par) * S:(2 + 2 * par) * S + S]

        aT = self.B[:, 3 * S:4 * S]

        def emit_scores(i):
            qt = i
            par = i % 2
            g = self.gsc
            s0 = 16 + 16 * par
            for ns in range(self.NT4):
                pslice = self.PS_A[:, ns * 512:ns * 512 + 512]
                mms = []
                for dc in range(DC):
                    mms.append(lambda E, o=pslice,
                               l=self.A_t(dc, qt * 128, 128),
                               r=self.A_t(8 + dc, ns * 512, 512),
                               st=(dc == 0), sp=(dc == DC - 1):
                               E.matmul(o, l, r, start=st, stop=sp))
                P.emit("t", mms, needs=["q", "k", "f:pA%d" % ns],
                       inc="pe", produces=["sc%d.%d" % (par, ns)])
                if ns % 2 == 1 or ns == self.NT4 - 1:
                    m = ns // 2
                    w2 = min(1024, S - m * 1024)
                    mneeds = ["sc%d.%d" % (par, ns)]
                    if ns % 2 == 1:
                        mneeds.append("sc%d.%d" % (par, ns - 1))
                    P.emit("v", lambda E, g=g, o=g[:, s0 + m:s0 + m + 1],
                           i_=self.PS_A[:, m * 1024:m * 1024 + w2]:
                           E.reduce_max(o, i_, axis=AX.X),
                           needs=mneeds, inc="dve",
                           produces=["smx%d.%d" % (par, m)])

        def emit_softmax(i):
            par = i % 2
            g = self.gsc
            s0 = 16 + 16 * par
            nm = (self.NT4 + 1) // 2
            P.emit("v", lambda E, g=g, s0=s0, nm=nm: E.reduce_max(
                g[:, s0 + 8:s0 + 9], g[:, s0:s0 + nm], axis=AX.X, negate=True),
                needs=["smx%d.%d" % (par, m) for m in range(nm)],
                inc="dve", produces=["nmax%d" % par])
            for m in range(nm):
                w2 = min(1024, S - m * 1024)
                fr = ["f:pA%d" % (2 * m)]
                if w2 > 512:
                    fr.append("f:pA%d" % (2 * m + 1))
                P.emit("s", lambda E, s=self, m=m, w2=w2,
                       o=raw(par)[:, m * 1024:m * 1024 + w2], g=g, s0=s0:
                       E.activation(o, s.PS_A[:, m * 1024:m * 1024 + w2],
                                    AF.Exp, bias=g[:, s0 + 8:s0 + 9], scale=1.0,
                                    accum_out=g[:, s0 + 9 + m:s0 + 10 + m]),
                       needs=["nmax%d" % par] +
                             (["f:raw%d" % par] if m == 0 else []),
                       inc="act",
                       produces=["e%d.%d" % (par, m)] + fr)
            if nm > 1:
                P.emit("v", lambda E, g=g, s0=s0, nm=nm: E.reduce_sum(
                    g[:, s0 + 14:s0 + 15], g[:, s0 + 9:s0 + 9 + nm], axis=AX.X),
                    needs=["e%d.%d" % (par, m) for m in range(nm)],
                    inc="dve", produces=["ssum%d" % par])
                P.emit("v", lambda E, g=g, s0=s0: E.reciprocal(
                    g[:, s0 + 15:s0 + 16], g[:, s0 + 14:s0 + 15]),
                    needs=["ssum%d" % par], inc="dve", produces=["rsum%d" % par])
            else:
                P.emit("v", lambda E, g=g, s0=s0: E.reciprocal(
                    g[:, s0 + 15:s0 + 16], g[:, s0 + 9:s0 + 10]),
                    needs=["e%d.0" % par], inc="dve", produces=["rsum%d" % par])
            P.emit("v", lambda E, o=scl(par)[:, 0:S], i_=raw(par)[:, 0:S],
                   g=g, s0=s0:
                   E.tensor_scalar(o, i_, g[:, s0 + 15:s0 + 16], None, ALU.mult),
                   needs=["rsum%d" % par, "f:scl%d" % par], inc="dve",
                   produces=["attnq%d" % par, "f:raw%d" % par])

        def emit_tail(i):
            qt = i
            par = i % 2
            for g0 in range(0, self.NT, 8):
                gn = min(8, self.NT - g0)
                bank = (g0 // 8) % 2
                tps = []
                for i_ in range(gn):
                    kc = g0 + i_
                    slot = bank * 8 + i_
                    tps.append(lambda E, o=self.pT(slot),
                               i=scl(par)[:, kc * 128:(kc + 1) * 128], s=self:
                               E.transpose(o, i, s.idb[:, :]))
                P.emit("t", tps, needs=["attnq%d" % par, "idb",
                                        "fT%dv" % bank, "fT%ds" % bank],
                       inc="pe", produces=["trp"])
                use_v = (i + g0 // 8) % 2 == 0
                cps = []
                for i_ in range(gn):
                    kc = g0 + i_
                    slot = bank * 8 + i_
                    if use_v:
                        cps.append(lambda E, o=aT[:, kc * 128:(kc + 1) * 128],
                                   i=self.pT(slot): E.tensor_copy(o, i))
                    else:
                        cps.append(lambda E, o=aT[:, kc * 128:(kc + 1) * 128],
                                   i=self.pT(slot): E.copy(o, i))
                P.emit("v" if use_v else "s", cps, needs=["trp"],
                       inc="dve" if use_v else "act",
                       produces=["fT%dv" % bank, "fT%ds" % bank] +
                                ["aTc%d" % (g0 + i_) for i_ in range(gn)])
            for bank in range(2):
                mms = []
                for hc in range(4 * bank, 4 * bank + 4):
                    pdst = self.PS_B[:, hc * 128:(hc + 1) * 128]
                    for kc in range(self.NT):
                        mms.append(lambda E, o=pdst,
                                   l=self.VT[:, kc * 1024 + hc * 128:
                                             kc * 1024 + hc * 128 + 128],
                                   r=aT[:, kc * 128:(kc + 1) * 128],
                                   st=(kc == 0), sp=(kc == self.NT - 1):
                                   E.matmul(o, l, r, start=st, stop=sp))
                P.emit("t", mms,
                       needs=["VT", "f:pB%d" % bank] +
                             ["aTc%d" % kc for kc in range(self.NT)],
                       inc="pe", produces=["ctxp%d" % bank])
                octx = self.A[:, 0:24 * S].rearrange("p (blk s) -> p blk s",
                                                     blk=24)[
                    :, 16 + 4 * bank:16 + 4 * bank + 4,
                    qt * 128:(qt + 1) * 128]
                ictx = self.PS_B[:, 4 * bank * 128:
                                 (4 * bank + 4) * 128].rearrange(
                    "p (a b) -> p a b", a=4)
                P.emit("s", lambda E, o=octx, i=ictx: E.copy(o, i),
                       needs=["ctxp%d" % bank], inc="act",
                       produces=["ctx", "f:pB%d" % bank] +
                                (["f:scl%d" % par] if bank == 1 else []))

        n = self.NT
        for i in range(n + 1):
            if i < n:
                emit_scores(i)
            if i > 0:
                emit_tail(i - 1)
            if i < n:
                emit_softmax(i)
        P.alias("f:wbuf0", "ctxp1")
        P.alias("f:wbuf1", "ctxp1")
        P.alias("f:w1q0", "ctxp1")
        P.alias("f:w1q1", "ctxp1")
        P.alias("f:w2b", "ctxp1")

    def p_attn_l1(self):
        P, S = self.P, self.S
        units = [(j, qt) for j in range(16) for qt in range(self.NT)]

        def raw(par):
            return self.B[:, par * S:par * S + S]

        def scl(par):
            return self.B[:, (2 + 2 * par) * S:(2 + 2 * par) * S + S]

        aT = self.B[:, 3 * S:4 * S]

        def emit_scores(i):
            j, qt = units[i]
            dc = j // 2
            boff = 64 * (j % 2)
            kn = (qt + 1) * 128
            nsl = (kn + 511) // 512
            par = i % 2
            g = self.gsc
            s0 = 16 + 16 * par
            base = 0
            b0 = 0
            for ns in range(nsl):
                w = min(512, kn - ns * 512)
                pslice = self.PS_A[:, base + ns * 512:base + ns * 512 + w]
                mms = [lambda E, o=pslice,
                       l=self.A[boff:boff + 64,
                                dc * S + qt * 128: dc * S + qt * 128 + 128],
                       r=self.A[boff:boff + 64,
                                (8 + dc) * S + ns * 512:
                                (8 + dc) * S + ns * 512 + w],
                       sp=(ns != nsl - 1):
                       E.matmul(o, l, r, start=True, stop=sp)]
                if ns == nsl - 1:
                    mms.append(lambda E, s=self,
                               o=self.PS_A[:, base + qt * 128:
                                           base + qt * 128 + 128]:
                               E.matmul(o, s.idb[:, :], s.mask[:, :],
                                        start=False, stop=True))
                P.emit("t", mms,
                       needs=["q", "k", "f:pA%d" % (b0 + ns)] +
                             (["idb", "mask"] if ns == nsl - 1 else []),
                       inc="pe", produces=["sc%d.%d" % (par, ns)])
                if ns % 2 == 1 or ns == nsl - 1:
                    m = ns // 2
                    w2 = min(1024, kn - m * 1024)
                    mneeds = ["sc%d.%d" % (par, ns)]
                    if ns % 2 == 1:
                        mneeds.append("sc%d.%d" % (par, ns - 1))
                    P.emit("v", lambda E, g=g, o=g[:, s0 + m:s0 + m + 1],
                           i_=self.PS_A[:, base + m * 1024:
                                        base + m * 1024 + w2]:
                           E.reduce_max(o, i_, axis=AX.X),
                           needs=mneeds, inc="dve",
                           produces=["smx%d.%d" % (par, m)])

        def emit_softmax(i):
            j, qt = units[i]
            kn = (qt + 1) * 128
            nsl = (kn + 511) // 512
            par = i % 2
            g = self.gsc
            s0 = 16 + 16 * par
            nm = (nsl + 1) // 2
            P.emit("v", lambda E, g=g, s0=s0, nm=nm: E.reduce_max(
                g[:, s0 + 8:s0 + 9], g[:, s0:s0 + nm], axis=AX.X, negate=True),
                needs=["smx%d.%d" % (par, m) for m in range(nm)],
                inc="dve", produces=["nmax%d" % par])
            base = 0
            b0 = 0
            for m in range(nm):
                w2 = min(1024, kn - m * 1024)
                fr = ["f:pA%d" % (b0 + 2 * m)]
                if w2 > 512:
                    fr.append("f:pA%d" % (b0 + 2 * m + 1))
                P.emit("s", lambda E, s=self, m=m, w2=w2, base=base,
                       o=raw(par)[:, m * 1024:m * 1024 + w2], g=g, s0=s0:
                       E.activation(o, s.PS_A[:, base + m * 1024:
                                              base + m * 1024 + w2],
                                    AF.Exp, bias=g[:, s0 + 8:s0 + 9], scale=1.0,
                                    accum_out=g[:, s0 + 9 + m:s0 + 10 + m]),
                       needs=["nmax%d" % par] +
                             (["f:raw%d" % par] if m == 0 else []),
                       inc="act",
                       produces=["e%d.%d" % (par, m)] + fr)
            if nm > 1:
                P.emit("v", lambda E, g=g, s0=s0, nm=nm: E.reduce_sum(
                    g[:, s0 + 14:s0 + 15], g[:, s0 + 9:s0 + 9 + nm], axis=AX.X),
                    needs=["e%d.%d" % (par, m) for m in range(nm)],
                    inc="dve", produces=["ssum%d" % par])
                P.emit("v", lambda E, g=g, s0=s0: E.reciprocal(
                    g[:, s0 + 15:s0 + 16], g[:, s0 + 14:s0 + 15]),
                    needs=["ssum%d" % par], inc="dve", produces=["rsum%d" % par])
            else:
                P.emit("v", lambda E, g=g, s0=s0: E.reciprocal(
                    g[:, s0 + 15:s0 + 16], g[:, s0 + 9:s0 + 10]),
                    needs=["e%d.0" % par], inc="dve", produces=["rsum%d" % par])
            P.emit("v", lambda E, o=scl(par)[:, 0:kn], i_=raw(par)[:, 0:kn],
                   g=g, s0=s0:
                   E.tensor_scalar(o, i_, g[:, s0 + 15:s0 + 16], None, ALU.mult),
                   needs=["rsum%d" % par, "f:scl%d" % par], inc="dve",
                   produces=["attnq%d" % par, "f:raw%d" % par])

        ctx_cnt = 0

        def emit_tail(i):
            nonlocal ctx_cnt
            j, qt = units[i]
            dc = j // 2
            boff = 64 * (j % 2)
            par = i % 2
            for g0 in range(0, qt + 1, 8):
                gn = min(8, qt + 1 - g0)
                bank = (g0 // 8) % 2
                tps = []
                for i_ in range(gn):
                    kc = g0 + i_
                    slot = bank * 8 + i_
                    tps.append(lambda E, o=self.pT(slot),
                               i=scl(par)[:, kc * 128:(kc + 1) * 128], s=self:
                               E.transpose(o, i, s.idb[:, :]))
                P.emit("t", tps, needs=["attnq%d" % par, "idb",
                                        "fT%dv" % bank, "fT%ds" % bank],
                       inc="pe", produces=["trp"])
                use_v = (i + g0 // 8) % 2 == 0
                cps = []
                for i_ in range(gn):
                    kc = g0 + i_
                    slot = bank * 8 + i_
                    if use_v:
                        cps.append(lambda E, o=aT[:, kc * 128:(kc + 1) * 128],
                                   i=self.pT(slot): E.tensor_copy(o, i))
                    else:
                        cps.append(lambda E, o=aT[:, kc * 128:(kc + 1) * 128],
                                   i=self.pT(slot): E.copy(o, i))
                P.emit("v" if use_v else "s", cps, needs=["trp"],
                       inc="dve" if use_v else "act",
                       produces=["fT%dv" % bank, "fT%ds" % bank] +
                                ["aTc%d" % (g0 + i_) for i_ in range(gn)])
            cslot = ctx_cnt % 2
            ctx_cnt += 1
            pdst = self.PS_B[0:64, cslot * 512:cslot * 512 + 128]
            mms = []
            for kc in range(qt + 1):
                mms.append(lambda E, o=pdst,
                           l=self.VT[:, kc * 1024 + 64 * j:
                                     kc * 1024 + 64 * j + 64],
                           r=aT[:, kc * 128:(kc + 1) * 128],
                           st=(kc == 0), sp=(kc == qt):
                           E.matmul(o, l, r, start=st, stop=sp))
            P.emit("t", mms,
                   needs=["VT", "f:pB%d" % cslot] +
                         ["aTc%d" % kc for kc in range(qt + 1)],
                   inc="pe", produces=["ctxp"])
            P.emit("s", lambda E,
                   o=self.A[boff:boff + 64,
                            (16 + dc) * S + qt * 128:
                            (16 + dc) * S + qt * 128 + 128],
                   i=pdst: E.copy(o, i),
                   needs=["ctxp"], inc="act",
                   produces=["ctx", "f:pB%d" % cslot, "f:scl%d" % par])

        n = len(units)
        for i in range(n + 1):
            if i < n:
                emit_scores(i)
            if i > 0:
                emit_tail(i - 1)
            if i < n:
                emit_softmax(i)
        P.alias("f:pB0", "ctx")
        P.alias("f:pB1", "ctx")
        P.alias("f:wbuf0", "ctxp")
        P.alias("f:wbuf1", "ctxp")
        P.alias("f:w1q0", "ctxp")
        P.alias("f:w1q1", "ctxp")
        P.alias("f:w2b", "ctxp")

    def p_ln(self, out_ap=None):
        P, S, HS = self.P, self.S, self.HS
        mrow = self.sm[0:1, 0:HS]
        rrow = self.sm[0:1, HS:2 * HS]
        m2row = self.sm[0:1, 2 * HS:3 * HS]
        sq = self.sm[:, 3 * 1024:3 * 1024 + min(HS, 1024)]
        inv_d = 1.0 / 1024.0
        nsl = (HS + 511) // 512
        for th in range(2):
            t0 = th * HS
            lastk = "xs" if out_ap is None else "stgc"
            if th > 0:
                for b in range(4):
                    P.alias("f:pA%d" % b, lastk)
                for b in range(2):
                    P.alias("f:pB%d" % b, lastk)
            for dc in range(DC):
                P.emit("v", lambda E, o=sq[:, 0:HS],
                       a=self.sb_t(dc, t0, HS):
                       E.tensor_tensor(o, a, a, ALU.mult),
                       needs=["sb", "f:sq", "f:tmp"], inc="dve", produces=["sqd"])
                for ns in range(nsl):
                    w = min(512, HS - ns * 512)
                    P.emit("t", lambda E,
                           o=self.PS_A[0:1, ns * 512:ns * 512 + w],
                           l=self.ones[:, 0:1],
                           r=self.sb_t(dc, t0 + ns * 512, w),
                           st=(dc == 0), sp=(dc == DC - 1):
                           E.matmul(o, l, r, start=st, stop=sp),
                           needs=(["sb", "ones", "f:pA0", "f:pA1"]
                                  if (dc == 0 and ns == 0) else ["sb"]),
                           inc="pe" if dc == DC - 1 else None,
                           produces=["psum"] if dc == DC - 1 else [])
                for ns in range(nsl):
                    w = min(512, HS - ns * 512)
                    P.emit("t", lambda E,
                           o=self.PS_A[0:1, 1024 + ns * 512: 1024 + ns * 512 + w],
                           l=self.ones[:, 0:1], r=sq[:, ns * 512:ns * 512 + w],
                           st=(dc == 0), sp=(dc == DC - 1):
                           E.matmul(o, l, r, start=st, stop=sp),
                           needs=(["sqd", "f:pA2", "f:pA3"]
                                  if (dc == 0 and ns == 0) else ["sqd"]),
                           inc="pe", produces=["sqrd"] + (["psq"] if dc == DC - 1 else []))
                P.alias("f:sq", "sqrd")
            P.emit("v", lambda E, o=mrow, s=self, iv=inv_d: E.tensor_scalar(
                o, s.PS_A[0:1, 0:HS], iv, None, ALU.mult),
                needs=["psum"], inc="dve", produces=["mrow", "f:pA0", "f:pA1"])
            P.emit("v", lambda E, s=self, iv=inv_d: E.tensor_scalar(
                s.PS_A[0:1, 1024:1024 + HS], s.PS_A[0:1, 1024:1024 + HS], iv, None, ALU.mult),
                needs=["psq"], inc="dve", produces=["msq"])
            P.emit("v", lambda E, o=m2row, i=mrow: E.tensor_tensor(o, i, i, ALU.mult),
                   needs=["mrow"], inc="dve", produces=["m2"])
            P.emit("v", lambda E, s=self, i=m2row: E.tensor_tensor(
                s.PS_A[0:1, 1024:1024 + HS], s.PS_A[0:1, 1024:1024 + HS], i, ALU.subtract),
                needs=["msq", "m2"], inc="dve", produces=["var"])
            P.emit("v", lambda E, s=self: E.tensor_scalar(
                s.PS_A[0:1, 1024:1024 + HS], s.PS_A[0:1, 1024:1024 + HS], 1e-5, None, ALU.add),
                needs=["var"], inc="dve", produces=["vare"])
            P.emit("s", lambda E, s=self: E.activation(
                s.PS_A[0:1, 1024:1024 + HS], s.PS_A[0:1, 1024:1024 + HS], AF.Ln),
                needs=["vare"], inc="act", produces=["lnv"])
            P.emit("s", lambda E, s=self, o=rrow: E.activation(
                o, s.PS_A[0:1, 1024:1024 + HS], AF.Exp, scale=-0.5),
                needs=["lnv"], inc="act", produces=["rrow", "f:pA2", "f:pA3"])
            for ns in range(nsl):
                w = min(512, HS - ns * 512)
                P.emit("t", lambda E, o=self.PS_B[:, ns * 512:ns * 512 + w],
                       l=self.ones[0:1, 0:128],
                       r=mrow[0:1, ns * 512:ns * 512 + w]:
                       E.matmul(o, l, r, start=True, stop=True),
                       needs=["mrow", "ones", "f:pB%d" % ns],
                       inc="pe", produces=["mbc%d" % ns])
                P.emit("t", lambda E, o=self.PS_A[:, ns * 512:ns * 512 + w],
                       l=self.ones[0:1, 0:128],
                       r=rrow[0:1, ns * 512:ns * 512 + w]:
                       E.matmul(o, l, r, start=True, stop=True),
                       needs=["rrow", "ones", "f:pA%d" % ns],
                       inc="pe", produces=["rbc%d" % ns])
            if out_ap is None:
                tmp = self.sm[:, 3 * 1024:3 * 1024 + 512]
                for dc in range(DC):
                    for nb in range(nsl):
                        w = min(512, HS - nb * 512)
                        P.emit("v", lambda E, o=tmp[:, 0:w],
                               a=self.sb_t(dc, t0 + nb * 512, w),
                               m=self.PS_B[:, nb * 512:nb * 512 + w]:
                               E.tensor_tensor(o, a, m, ALU.subtract),
                               needs=["mbc%d" % nb, "sb", "f:tmp", "sqrd"],
                               inc="dve", produces=["tmpd"])
                        P.emit("v", lambda E,
                               o=self.xs_t(dc, t0 + nb * 512, w),
                               a=tmp[:, 0:w],
                               r=self.PS_A[:, nb * 512:nb * 512 + w]:
                               E.tensor_tensor(o, a, r, ALU.mult),
                               needs=["rbc%d" % nb, "tmpd"],
                               inc="dve", produces=["xs", "f:tmp"])
            else:
                for tcl in range(HS // 128):
                    tc = th * (HS // 128) + tcl
                    roomy = (DC * self.S // 2) >= 3072
                    sp = (tc % 2) if roomy else 0
                    stage = self.xs_f32[:, sp * 1024:sp * 1024 + 1024]
                    fb = 2048 if roomy else 1024
                    fxy8 = self.xs_f32[:, fb:fb + 1024]
                    nb = (tcl * 128) // 512
                    ci = tcl * 128 - nb * 512
                    subs, mults = [], []
                    for dc in range(DC):
                        fx = fxy8[:, dc * 128:(dc + 1) * 128]
                        subs.append(lambda E, o=fx,
                                    a=self.sb_t(dc, t0 + tcl * 128, 128),
                                    m=self.PS_B[:, nb * 512 + ci:
                                                nb * 512 + ci + 128]:
                                    E.tensor_tensor(o, a, m, ALU.subtract))
                        mults.append(lambda E, o=fx, a=fx,
                                     r=self.PS_A[:, nb * 512 + ci:
                                                 nb * 512 + ci + 128]:
                                     E.tensor_tensor(o, a, r, ALU.mult))
                    P.emit("v", subs, needs=["mbc%d" % nb, "sb", "f:fx"],
                           inc="dve", produces=["fxs"])
                    P.emit("v", mults, needs=["rbc%d" % nb, "fxs"],
                           inc="dve", produces=["fxm"])
                    tps = []
                    for dc in range(DC):
                        fx = fxy8[:, dc * 128:(dc + 1) * 128]
                        pdst = self.PS_A[:, 1024 + dc * 128:1024 + dc * 128 + 128]
                        tps.append(lambda E, o=pdst, i=fx, s=self:
                                   E.transpose(o, i, s.idf[:, :]))
                    P.emit("t", tps, needs=["fxm", "idf", "f:ot"],
                           inc="pe", produces=["otr"])
                    cps = []
                    for dc in range(DC):
                        pdst = self.PS_A[:, 1024 + dc * 128:1024 + dc * 128 + 128]
                        cps.append(lambda E, o=stage[:, dc * 128:(dc + 1) * 128],
                                   i=pdst: E.tensor_copy(o, i))
                    P.emit("v", cps, needs=["otr", "f:stgo%d" % sp], inc="dve",
                           produces=["f:ot", "stgc", "f:fx"])
                    if sp == 0:
                        P.emit("y", lambda E, o=out_ap[tc * 128:(tc + 1) * 128, :],
                               i=stage: E.dma_start(o, i),
                               needs=["stgc"], inc="hw",
                               produces=["f:stgo%d" % sp], amt=16)
                    else:
                        P.emit("g", lambda E, o=out_ap[tc * 128:(tc + 1) * 128, :],
                               i=stage: E.dma_start(o, i),
                               needs=["stgc"], inc="ld",
                               produces=["f:stgo%d" % sp], amt=16)
        P.alias("f:sm_rows", "rbc%d" % (nsl - 1))
        lastk = "xs" if out_ap is None else "stgc"
        for b in range(4):
            P.alias("f:pA%d" % b, lastk)
        for b in range(2):
            P.alias("f:pB%d" % b, lastk)
        if out_ap is None:
            P.alias("f:sm_tmp", "xs")
        else:
            P.alias("f:sm_tmp", "rbc%d" % (nsl - 1))
            P._wait("y", "hw", P.cnt["hw"])
            P._wait("g", "ld", P.cnt["ld"])

    def p_gate(self, l1, pfx):
        P, S, NT = self.P, self.S, self.NT
        if not l1:
            rw = self.dram_in("l0_rwT", [D, 8])
            rb = self.dram_in("l0_rb", [1, 8])
            lw = self.dram_in("l0_lwT", [D, 4])
            lb = self.dram_in("l0_lb", [1, 4])
            P.emit("g", lambda E, s=self,
                   a=rw.rearrange("(dc p) e -> p dc e", p=128):
                   E.dma_start(s.rwsb[:, :].rearrange("p (dc e) -> p dc e", dc=8), a),
                   inc="ld", produces=["rw"], amt=16)
            P.emit("g", lambda E, s=self,
                   a=lw.rearrange("(dc p) e -> p dc e", p=128):
                   E.dma_start(s.lwsb[:, :].rearrange("p (dc e) -> p dc e", dc=8), a),
                   inc="ld", produces=["lwk"], amt=16)
            P.emit("g", lambda E, s=self, a=rb: E.dma_start(s.brow[:, :], a),
                   inc="ld", produces=["brow"], amt=16)
            P.emit("g", lambda E, s=self, a=lb: E.dma_start(s.lrow[:, :], a),
                   inc="ld", produces=["lrow"], amt=16)
            wsb, wk, brow_sb, bk = self.rwsb, "rw", self.brow, "brow"
        else:
            wgap = self.dram_in("l1_wg", [D, 8])
            bg = self.dram_in("l1_bg", [1, 8])
            P.emit("g", lambda E, s=self,
                   a=wgap.rearrange("(dc p) e -> p dc e", p=128):
                   E.dma_start(s.wgsb[:, :].rearrange("p (dc e) -> p dc e", dc=8), a),
                   inc="ld", produces=["wgk"], amt=16)
            P.emit("g", lambda E, s=self, a=bg: E.dma_start(s.bgrow[:, :], a),
                   inc="ld", produces=["bgrow"], amt=16)
            wsb, wk, brow_sb, bk = self.wgsb, "wgk", self.bgrow, "bgrow"

        Bf = self.B_f32
        RL = Bf[:, 0:NT * 8].rearrange("p (t e) -> p t e", t=NT)
        EW = Bf[:, NT * 8:NT * 16].rearrange("p (t e) -> p t e", t=NT)
        PE_ = Bf[:, NT * 16:NT * 24].rearrange("p (t e) -> p t e", t=NT)
        PR = Bf[:, NT * 24:NT * 32].rearrange("p (t e) -> p t e", t=NT)
        MK = Bf[:, NT * 32:NT * 40].rearrange("p (t e) -> p t e", t=NT)
        GA = Bf[:, NT * 40:NT * 48].rearrange("p (t e) -> p t e", t=NT)
        LE = Bf[:, NT * 48:NT * 52].rearrange("p (t e) -> p t e", t=NT)
        LP = Bf[:, NT * 52:NT * 56].rearrange("p (t e) -> p t e", t=NT)
        SC1 = Bf[:, NT * 56:NT * 57]
        SC2 = Bf[:, NT * 57:NT * 58]
        SC3 = Bf[:, NT * 58:NT * 59]
        b1 = lambda x: x.rearrange("p t -> p t ()") if False else x

        def bc(x2d, e):
            return x2d.rearrange("p (t o) -> p t o", o=1).broadcast_to(
                [128, NT, e])

        for tt in range(NT):
            mms = []
            for dc in range(DC):
                mms.append(lambda E, o=self.PS_A[:, tt * 8:tt * 8 + 8],
                           l=self.xs_t(dc, tt * 128, 128),
                           r=wsb[:, dc * 8:dc * 8 + 8],
                           st=(dc == 0): E.matmul(o, l, r, start=st, stop=False))
            mms.append(lambda E, o=self.PS_A[:, tt * 8:tt * 8 + 8], s=self,
                       br=brow_sb: E.matmul(o, s.ones[0:1, 0:128],
                                            br[0:1, 0:8], start=False, stop=True))
            P.emit("t", mms, needs=["xs", wk, bk, "ones",
                                    "f:pA0", "f:pA1"] if tt == 0 else ["xs"],
                   inc="pe", produces=["rlp"] if tt == NT - 1 else [])
        if not l1:
            for tt in range(NT):
                mms = []
                for dc in range(DC):
                    mms.append(lambda E,
                               o=self.PS_A[:, 1024 + tt * 4:1024 + tt * 4 + 4],
                               l=self.xs_t(dc, tt * 128, 128),
                               r=self.lwsb[:, dc * 4:dc * 4 + 4],
                               st=(dc == 0): E.matmul(o, l, r, start=st,
                                                      stop=False))
                mms.append(lambda E,
                           o=self.PS_A[:, 1024 + tt * 4:1024 + tt * 4 + 4],
                           s=self: E.matmul(o, s.ones[0:1, 0:128],
                                            s.lrow[0:1, 0:4],
                                            start=False, stop=True))
                P.emit("t", mms, needs=(["lwk", "lrow", "f:pA2", "f:pA3"]
                                        if tt == 0 else ["xs"]),
                       inc="pe", produces=["lgp"] if tt == NT - 1 else [])

        def ch(eng, fn, needs=(), produces=()):
            sem = {"v": "dve", "s": "act"}[eng]
            P.emit(eng, fn, needs=list(needs) + ["ch"], inc=sem,
                   produces=list(produces) + ["ch"])

        if "ch" not in P.last:
            P.last["ch"] = P.last["rlp"]
        psRL = self.PS_A[:, 0:NT * 8].rearrange("p (t e) -> p t e", t=NT)
        if not l1:
            psLG = self.PS_A[:, 1024:1024 + NT * 4].rearrange(
                "p (t e) -> p t e", t=NT)
            ch("s", lambda E: E.activation(LE, psLG, AF.Exp),
               needs=["lgp", "psum", "psq", "xs"])
            ch("v", lambda E: E.tensor_reduce(SC1, LE, AX.X, ALU.add),
               produces=["f:pA2", "f:pA3"])
            ch("v", lambda E: E.reciprocal(SC2, SC1))
            ch("v", lambda E: E.scalar_tensor_tensor(LP, LE, 0.1, bc(SC2, 4),
                                                     ALU.mult, ALU.mult))
            ch("v", lambda E: E.tensor_copy(EW[:, :, 0:4], LP))
            ch("v", lambda E: E.tensor_copy(EW[:, :, 4:8], LP))
            ch("v", lambda E: E.tensor_tensor(EW[:, :, 0:3], EW[:, :, 0:3],
                                              LP[:, :, 1:4], ALU.add))
            ch("v", lambda E: E.tensor_tensor(EW[:, :, 4:7], EW[:, :, 4:7],
                                              LP[:, :, 1:4], ALU.add))
            ch("v", lambda E: E.tensor_tensor(RL, psRL, EW, ALU.add),
               needs=["rlp"], produces=["f:pA0", "f:pA1"])
        else:
            ch("v", lambda E: E.tensor_copy(RL, psRL),
               needs=["rlp", "psum", "psq", "xs"],
               produces=["f:pA0", "f:pA1"])
        ch("s", lambda E: E.activation(PE_, RL, AF.Exp))
        ch("v", lambda E: E.tensor_reduce(SC1, PE_, AX.X, ALU.add))
        ch("v", lambda E: E.reciprocal(SC2, SC1))
        ch("v", lambda E: E.tensor_tensor(PR, PE_, bc(SC2, 8), ALU.mult))
        ch("v", lambda E: E.tensor_reduce(SC1, RL, AX.X, ALU.max))
        ch("v", lambda E: E.tensor_tensor(MK, RL, bc(SC1, 8), ALU.is_ge))
        if not l1:
            ch("v", lambda E: E.scalar_tensor_tensor(MK, MK, -30000.0, RL,
                                                     ALU.mult, ALU.add))
            ch("v", lambda E: E.tensor_reduce(SC3, MK, AX.X, ALU.max))
            ch("v", lambda E: E.tensor_tensor(MK, RL, bc(SC3, 8), ALU.is_ge))
            ch("v", lambda E: E.tensor_tensor(GA, PR, MK, ALU.mult))
            ch("v", lambda E: E.tensor_reduce(SC1, GA, AX.X, ALU.add))
            ch("v", lambda E: E.reciprocal(SC2, SC1))
            ch("v", lambda E: E.tensor_tensor(GA, GA, bc(SC2, 8), ALU.mult))
        else:
            ch("v", lambda E: E.tensor_tensor(GA, PR, MK, ALU.mult))
        ch("v", lambda E, s=self, NT=NT: E.tensor_copy(
            s.gtb[:, 0:NT * 8].rearrange("p (t e) -> p t e", t=NT), GA),
           produces=["gtbd"])
        for tt in range(NT):
            P.emit("t", lambda E, s=self, tt=tt: E.transpose(
                s.PS_T[0:8, 0:128], s.gtb[:, tt * 8:tt * 8 + 8], s.idb[:, :]),
                needs=["gtbd", "idb", "fT0v", "fT0s"], inc="pe",
                produces=["gtr"])
            P.emit("v", lambda E, s=self, c0=tt * 128: E.tensor_copy(
                s.gfm_t(c0, 128), s.PS_T[0:8, 0:128]),
                needs=["gtr"], inc="dve",
                produces=["gfm", "fT0", "fT0v", "fT0s", "ch"])

    def p_ffn(self, l1, pfx):
        P, S = self.P, self.S
        if not l1:
            w1_ap = self.dram_in("l0_w1", [8, D, 4096])
            b1_ap = self.dram_in("l0_b1", [8, 4096])
            w2_ap = self.dram_in("l0_w2", [8, 4096, D])
            b2_ap = self.dram_in("l0_b2", [8, D])
            HF, nhalf = 4096, 2
            b1off = 0
        else:
            w1_ap = self.dram_in("l1_w1", [8, D, D])
            b1_ap = self.dram_in("l1_b1", [8, D])
            w2_ap = self.dram_in("l1_w2", [8, D, D])
            b2_ap = self.dram_in("l1_b2", [8, D])
            HF, nhalf = 1024, 1
            b1off = 256
        hfc = HF // 128 // nhalf
        hw = HF // nhalf
        P.emit("g", lambda E, s=self, a=b2_ap: E.dma_start(s.b2s[:, :], a),
               needs=["f:b2s"], inc="ld", produces=["b2s"], amt=16)
        b1sb = self.sm[0:8, 0:HF]
        P.emit("g", lambda E, o=b1sb, a=b1_ap: E.dma_start(o, a),
               needs=["f:b1sb", "f:stg0", "f:stg1", "f:sm_rows", "f:sm_tmp"],
               inc="ld", produces=["b1sb"], amt=16)
        for c in range(HF // 128):
            P.emit("t", lambda E, s=self, i=b1sb[0:8, c * 128:(c + 1) * 128]:
                   E.transpose(s.PS_T[0:128, 0:8], i, s.idb[0:8, 0:8]),
                   needs=["b1sb", "idb", "fT0v", "fT0s"], inc="pe",
                   produces=["b1tr"])
            P.emit("v", lambda E, s=self, c0=b1off + c * 8: E.tensor_copy(
                s.b1T[:, c0:c0 + 8], s.PS_T[0:128, 0:8]),
                needs=["b1tr"], inc="dve",
                produces=["b1T", "fT0", "fT0v", "fT0s"])
        ACC = self.A_f32
        for oc in range(DC):
            for nt in range(self.NT4):
                pk = "pB%d" % (nt % 2)
                pslice = self.PS_B[:, (nt % 2) * 512:(nt % 2) * 512 + 512]
                P.emit("t", lambda E, o=pslice,
                       l=self.b2s[0:8, oc * 128:(oc + 1) * 128],
                       r=self.gfm_t(nt * 512, 512):
                       E.matmul(o, l, r, start=True, stop=True),
                       needs=["b2s", "gfm", "f:" + pk], inc="pe",
                       produces=["b2m"])
                P.emit("v", lambda E,
                       o=ACC[:, oc * S + nt * 512: oc * S + nt * 512 + 512],
                       i=pslice: E.tensor_copy(o, i),
                       needs=["b2m"], inc="dve",
                       produces=["AC%d.%d" % (oc, nt), "f:" + pk])
        g_bc = self.B_f32[:, 4096:4096 + S]
        tmpb = 16 * 512 + 2 * S
        for e in range(8):
            for nt in range(self.NT4):
                P.emit("t", lambda E, s=self, o=self.PS_A[:, 0:512],
                       l=self.sel[0:8, e * 128:(e + 1) * 128],
                       r=self.gfm_t(nt * 512, 512):
                       E.matmul(o, l, r, start=True, stop=True),
                       needs=["sel", "gfm", "f:pA0"], inc="pe", produces=["gbp"])
                P.emit("v", lambda E, o=g_bc[:, nt * 512:nt * 512 + 512],
                       i=self.PS_A[:, 0:512]: E.tensor_copy(o, i),
                       needs=["gbp", "gbcrd", "xs"], inc="dve",
                       produces=["gbc", "f:pA0"])
            for half in range(nhalf):
                hq = hw // 2
                for q in range(2):
                    w1src = w1_ap[e, :, half * hw + q * hq:
                                  half * hw + q * hq + hq].rearrange(
                        "(dc p) h -> p dc h", p=128)
                    w1dst = self.A[:, 2 * DC * S: 2 * DC * S + DC * hw].rearrange(
                        "p (dc h) -> p dc h", dc=DC)[:, :, q * hq:(q + 1) * hq]
                    P.emit("g", lambda E, o=w1dst, a=w1src: E.dma_start(o, a),
                           needs=["f:w1q%d" % q], inc="ldw",
                           produces=["w1q%d" % q], amt=16)
                w2src = w2_ap[e, half * hw:half * hw + hw, :].rearrange(
                    "(hc p) d -> p hc d", p=128)
                w2dst = self.VT[:, 0:hfc * 1024].rearrange(
                    "p (hc d) -> p hc d", hc=hfc)
                P.emit("g", lambda E, o=w2dst, a=w2src: E.dma_start(o, a),
                       needs=["f:w2b", "f:wbuf0", "f:wbuf1"],
                       inc="ldw2", produces=["w2b"], amt=16)
                for tt in range(self.NT4):
                    for hc in range(hfc):
                        pk = "pA%d" % (hc % 2)
                        pslice = self.PS_A[:, (hc % 2) * 512:(hc % 2) * 512 + 512]
                        mms = []
                        for dc in range(DC):
                            mms.append(lambda E, o=pslice,
                                       l=self.A[:, 2 * DC * S + dc * hw + hc * 128:
                                                2 * DC * S + dc * hw + hc * 128 + 128],
                                       r=self.xs_t(dc, tt * 512, 512),
                                       st=(dc == 0), sp=(dc == DC - 1):
                                       E.matmul(o, l, r, start=st, stop=sp))
                        qq = 0 if hc < hfc // 2 else 1
                        prod = ["hp%d" % (hc % 2)]
                        if tt == self.NT4 - 1 and hc == hfc // 2 - 1:
                            prod.append("w1q0rd")
                        if tt == self.NT4 - 1 and hc == hfc - 1:
                            prod.append("w1q1rd")
                        P.emit("t", mms, needs=["w1q%d" % qq, "xs", "f:" + pk],
                               inc="pe", produces=prod)
                        hcg = half * hfc + hc
                        tmp = self.B[:, tmpb + (hc % 2) * 512:
                                     tmpb + (hc % 2) * 512 + 512]
                        P.emit("s", lambda E, o=tmp, i=pslice, s=self,
                               bc=b1off + hcg * 8 + e:
                               E.activation(o, i, AF.Gelu,
                                            bias=s.b1T[:, bc:bc + 1], scale=1.0),
                               needs=["hp%d" % (hc % 2), "b1T",
                                      "f:tm%d" % (hc % 2)],
                               inc="act", produces=["ge%d" % (hc % 2), "f:" + pk])
                        P.emit("v", lambda E, o=self.B[:, hc * 512:(hc + 1) * 512],
                               i=tmp, gb=g_bc[:, tt * 512:tt * 512 + 512]:
                               E.tensor_tensor(o, i, gb, ALU.mult),
                               needs=["ge%d" % (hc % 2), "gbc", "f:hsb"],
                               inc="dve",
                               produces=["h%d" % hc, "f:tm%d" % (hc % 2)])
                    for oc in range(DC):
                        pk = "pB%d" % (oc % 2)
                        pslice = self.PS_B[:, (oc % 2) * 512:(oc % 2) * 512 + 512]
                        mms = []
                        for hc in range(hfc):
                            mms.append(lambda E, o=pslice,
                                       l=self.VT[:, hc * 1024 + oc * 128:
                                                 hc * 1024 + oc * 128 + 128],
                                       r=self.B[:, hc * 512:(hc + 1) * 512],
                                       st=(hc == 0), sp=(hc == hfc - 1):
                                       E.matmul(o, l, r, start=st, stop=sp))
                        P.emit("t", mms,
                               needs=["w2b", "f:" + pk] +
                                     ["h%d" % hc for hc in range(hfc)],
                               inc="pe", produces=["op"])
                        P.emit("v", lambda E,
                               o=ACC[:, oc * S + tt * 512: oc * S + tt * 512 + 512],
                               i=pslice: E.tensor_tensor(o, i, o, ALU.add),
                               needs=["op", "AC%d.%d" % (oc, tt)], inc="dve",
                               produces=["AC%d.%d" % (oc, tt), "f:" + pk])
                    P.alias("f:hsb", "op")
                P.alias("f:w1q0", "w1q0rd")
                P.alias("f:w1q1", "w1q1rd")
                P.alias("f:w2b", "op")
            P.alias("gbcrd", "op")
        for dc in range(DC):
            for nt in range(self.NT4):
                P.emit("v", lambda E, o=self.sb_t(dc, nt * 512, 512),
                       a=ACC[:, dc * S + nt * 512: dc * S + nt * 512 + 512],
                       x=self.xs_t(dc, nt * 512, 512):
                       E.tensor_tensor(o, a, x, ALU.add),
                       needs=["AC%d.%d" % (dc, nt), "xs"], inc="dve",
                       produces=["sb"])
        P.alias("f:b2s", "b2m")
        P.alias("f:b1sb", "b1tr")
        P.alias("f:wbuf0", "op")
        P.alias("f:wbuf1", "op")


    def emit_l0a(self):
        P = self.P
        WqT = self.dram_in("l0_WqT", [D, D])
        WkT = self.dram_in("l0_WkT", [D, D])
        WvT = self.dram_in("l0_WvT", [D, D])
        WoT = self.dram_in("l0_WoT", [D, D])
        xsrc = lambda dc, c0, w: self.xs_t(dc, c0, w)
        for sl, (tag, wap, b0, dk) in enumerate((("0q", WqT, 0, "q"),
                                                 ("0k", WkT, 8, "k"),
                                                 ("0v", WvT, 16, "v"))):
            self.p_proj(wap, tag, xsrc, "xs",
                        dst_t=(lambda oc, c0, w, b=b0: self.A_t(b + oc, c0, w)),
                        dst_key=dk, slot=sl % 2,
                        out_scale=(1.0 / 32.0 if dk == "q" else None))
        self.p_build_vt("v")
        self.p_attn_l0()
        self.p_proj(WoT, "0o", (lambda dc, c0, w: self.A_t(16 + dc, c0, w)),
                    "ctx", resid=True, slot=1)

    def emit_l1a(self):
        P = self.P
        Wa = self.dram_in("l1_Wa", [D, 3 * D])
        ba = self.dram_in("l1_ba", [24, 128])
        Wp = self.dram_in("l1_Wp", [D, D])
        bp = self.dram_in("l1_bp", [8, 128])
        stg = self.sm[0:24, 0:128]
        P.emit("g", lambda E, o=stg, a=ba: E.dma_start(o, a),
               needs=["f:b1sb", "f:stg0", "f:stg1", "f:sm_rows", "f:sm_tmp"],
               inc="ld", produces=["bast"], amt=16)
        P.emit("t", lambda E, s=self, i=stg: E.transpose(
            s.PS_T[0:128, 0:24], i, s.idb[0:24, 0:24]),
            needs=["bast", "idb", "fT0v", "fT0s"], inc="pe",
            produces=["batr"])
        P.emit("v", lambda E, s=self: E.tensor_copy(
            s.baT[:, 0:24], s.PS_T[0:128, 0:24]),
            needs=["batr"], inc="dve",
            produces=["baT", "fT0", "fT0v", "fT0s"])
        stg2 = self.sm[0:8, 128:256]
        P.emit("g", lambda E, o=stg2, a=bp: E.dma_start(o, a),
               needs=["f:b1sb", "f:stg0", "f:stg1", "f:sm_rows", "f:sm_tmp"],
               inc="ld", produces=["bpst"], amt=16)
        P.emit("t", lambda E, s=self, i=stg2: E.transpose(
            s.PS_T[0:128, 0:8], i, s.idb[0:8, 0:8]),
            needs=["bpst", "idb", "fT0v", "fT0s"], inc="pe",
            produces=["bptr"])
        P.emit("v", lambda E, s=self: E.tensor_copy(
            s.baT[:, 24:32], s.PS_T[0:128, 0:8]),
            needs=["bptr"], inc="dve",
            produces=["bpT", "fT0", "fT0v", "fT0s"])
        P.alias("f:b1sb", "bptr")
        xsrc = lambda dc, c0, w: self.xs_t(dc, c0, w)
        for sec, (tag, b0, dk) in enumerate((("1q", 0, "q"), ("1k", 8, "k"),
                                             ("1v", 16, "v"))):
            wsec = Wa[:, sec * D:(sec + 1) * D]
            self.p_proj(wsec, tag, xsrc, "xs",
                        dst_t=(lambda oc, c0, w, b=b0: self.A_t(b + oc, c0, w)),
                        dst_key=dk,
                        bias_col=(lambda oc, s0=sec * 8: self.baT[:, s0 + oc:s0 + oc + 1]),
                        bias_key="baT", slot=sec % 2,
                        out_scale=(0.125 if dk == "q" else None))
        self.p_build_vt("v")
        self.p_attn_l1()
        self.p_proj(Wp, "1o", (lambda dc, c0, w: self.A_t(16 + dc, c0, w)),
                    "ctx", resid=True,
                    bias_col=(lambda oc: self.baT[:, 24 + oc:24 + oc + 1]),
                    bias_key="bpT", slot=1)

    def finalize(self):
        P = self.P
        nc = self.nc
        with nc.Block() as block:
            @block.tensor
            def _(E):
                for f in P.ops["t"]:
                    f(E)

            @block.vector
            def _(E):
                for f in P.ops["v"]:
                    f(E)

            @block.scalar
            def _(E):
                for f in P.ops["s"]:
                    f(E)

            @block.gpsimd
            def _(E):
                for f in P.ops["g"]:
                    f(E)

            @block.sync
            def _(E):
                for f in P.ops["y"]:
                    f(E)
        self.es.close()
        return nc


def build_program(parts, S):
    kb = KB(S)
    kb.alloc()
    need_mask = "l1a" in parts
    need_sel = ("l0f" in parts) or ("l1f" in parts)
    kb.p_consts(need_mask, need_sel, need_idf=True)
    kb.p_load_x()
    out_ap = kb.dram_out("x_out", [S, D])
    for i, part in enumerate(parts):
        last = (i == len(parts) - 1)
        if part == "l0a":
            kb.emit_l0a()
        elif part == "l0f":
            kb.p_gate(False, "0")
            kb.p_ffn(False, "0")
        elif part == "l1a":
            kb.emit_l1a()
        elif part == "l1f":
            kb.p_gate(True, "1")
            kb.p_ffn(True, "1")
        kb.p_ln(out_ap=out_ap if last else None)
    return kb



def _prep(inputs):
    f = lambda a: np.ascontiguousarray(a, dtype=np.float32)
    w = {}
    W_qkv = inputs["L0_attn_W_qkv"]
    w["l0_WqT"] = f(W_qkv[0::3, :].T)
    w["l0_WkT"] = f(W_qkv[1::3, :].T)
    w["l0_WvT"] = f(W_qkv[2::3, :].T)
    w["l0_WoT"] = f(inputs["L0_attn_W_out"].T)
    w["l0_rwT"] = f(inputs["L0_ffn_router_w"].T)
    w["l0_rb"] = f(inputs["L0_ffn_router_b"].reshape(1, 8))
    w["l0_lwT"] = f(inputs["L0_ffn_lang_w"].T)
    w["l0_lb"] = f(inputs["L0_ffn_lang_b"].reshape(1, 4))
    w["l0_w1"] = f(inputs["L0_ffn_w1"])
    w["l0_b1"] = f(inputs["L0_ffn_b1"])
    w["l0_w2"] = f(inputs["L0_ffn_w2"])
    w["l0_b2"] = f(inputs["L0_ffn_b2"])
    w["l1_Wa"] = f(inputs["L1_attn_W_attn"])
    w["l1_ba"] = f(inputs["L1_attn_b_attn"].reshape(24, 128))
    w["l1_Wp"] = f(inputs["L1_attn_W_proj"])
    w["l1_bp"] = f(inputs["L1_attn_b_proj"].reshape(8, 128))
    w["l1_wg"] = f(inputs["L1_ffn_Wg"])
    w["l1_bg"] = f(inputs["L1_ffn_bg"].reshape(1, 8))
    w["l1_w1"] = f(inputs["L1_ffn_W1"])
    w["l1_b1"] = f(inputs["L1_ffn_b1"])
    w["l1_w2"] = f(inputs["L1_ffn_W2"])
    w["l1_b2"] = f(inputs["L1_ffn_b2"])
    ii, jj = np.meshgrid(np.arange(128), np.arange(128), indexing="ij")
    w["c_mask"] = np.where(jj <= ii, 0.0, -30000.0).astype(np.float32)
    w["c_ident"] = np.eye(128, dtype=np.float32)
    sel = np.zeros((8, 1024), dtype=np.float32)
    for e in range(8):
        sel[e, e * 128:(e + 1) * 128] = 1.0
    w["c_sel"] = sel
    return w


_STAGE_PARTS = {
    "staged4": [["l0a"], ["l0f"], ["l1a"], ["l1f"]],
    "fused": [["l0a", "l0f", "l1a", "l1f"]],
}


def kernel(**inputs):
    w = _prep(inputs)
    x = np.ascontiguousarray(inputs["L0_attn_x"], dtype=np.float32)
    B = x.shape[0]
    xcur = [np.ascontiguousarray(x[c]) for c in range(B)]
    for parts in _STAGE_PARTS[MODE]:
        kb = build_program(parts, S_FULL)
        nc = kb.finalize()
        names = set(kb.inp.keys())
        in_maps = []
        for c in range(B):
            m = {k: w[k] for k in names if k in w}
            if "x_in" in names:
                m["x_in"] = xcur[c]
            in_maps.append(m)
        res = run_bass_kernel_spmd(nc, in_maps, core_ids=list(range(NCORES)))
        xcur = [np.ascontiguousarray(res.results[c]["x_out"]) for c in range(B)]
    return np.stack(xcur, axis=0)
